# version 10
# baseline (speedup 1.0000x reference)
import numpy as np
import ml_dtypes
from contextlib import ExitStack

import concourse.bass as bass
import concourse.mybir as mybir
from concourse.bass_utils import run_bass_kernel_spmd

F32 = mybir.dt.float32
BF16 = mybir.dt.bfloat16


class _PIns:
    def __init__(self, owner, ins):
        self._o = owner
        self._ins = ins

    def then_inc(self, sem, n):
        o = self._o
        key = id(sem)
        o.counts[key] = o.counts.get(key, 0) + n
        o.pend = None
        o.last = (sem, o.counts[key])
        self._ins.then_inc(sem, n)
        return self._ins


class SEng:
    def __init__(self, e, selfsem):
        self.e = e
        self.selfsem = selfsem
        self.counts = {}
        self.pend = None
        self.last = None

    def _flush(self):
        if self.pend is not None:
            key = id(self.selfsem)
            self.counts[key] = self.counts.get(key, 0) + 1
            self.pend.then_inc(self.selfsem, 1)
            self.last = (self.selfsem, self.counts[key])
            self.pend = None

    def wait_ge(self, sem, val):
        self._flush()
        self.e.wait_ge(sem, val)

    def __getattr__(self, name):
        fn = getattr(self.e, name)

        def wrapped(*a, _nowait=False, **k):
            self._flush()
            if self.last is not None and not _nowait:
                self.e.wait_ge(*self.last)
            ins = fn(*a, **k)
            self.pend = ins
            return _PIns(self, ins)

        return wrapped
AX = mybir.AxisListType.X
ALU = mybir.AluOpType
ACT = mybir.ActivationFunctionType

S_LEN, E, H, D, F3, FF = 8192, 1024, 16, 64, 3072, 4096
NT = 64
EPS = 1e-5
N_WQ, N_WO, N_W1, N_W2 = 24, 8, 32, 32
NPRE = N_WQ + N_WO
NSUM = N_W1 + N_W2
NQNT = N_W1 + N_W2

O_W1T = 0
O_WQT = 32768
O_TT = O_WQT + 24576
O_QKV = O_TT + 1024
O_W2T = 32768
O_QP = 65536
O_KP = O_QP + 1024
O_VP = O_KP + 1024
O_QT = O_VP + 1040
O_KT = O_QT + 2048
O_S = O_KT + 2048
O_AV = O_S + 2048
O_OSB = O_AV + 2048
O_TNAT = O_OSB + 1024
B_HG = 65536
B_OUT2 = B_HG + 4096
B_O2T = B_OUT2 + 1024
B_UQT = B_O2T + 1024
B_VQT = B_UQT + 1024
B_JNK = B_VQT + 2048
B_SGN = B_JNK + 512
O_SP = 78912
O_SM = O_SP + 1024
O_W2S = O_SM + 1024
O_MSK = O_W2S + 1024
O_JNKA = O_MSK + 512
B_XB = 78912
B_INT = 79936
B_SCR = 80960
O_WOT = 83520
R_ROWS = O_WOT + 8192
ARENA_COLS = R_ROWS + 4096
O_XF = 0
O_STAG = 4096
B_Z = 0
B_OST = 2048
B_Y0 = 4096
B_RES = 6144
F_SCR = 8192
ARENA2_F32 = (F_SCR + 1024) // 2


def build_program():
    nc = bass.Bass("TRN2", target_bir_lowering=False)
    dx = nc.dram_tensor("x", [S_LEN, E], F32, kind="ExternalInput").ap()
    dwq = nc.dram_tensor("wq", [F3, E], F32, kind="ExternalInput").ap()
    dbq = nc.dram_tensor("bq", [1, F3], F32, kind="ExternalInput").ap()
    dwo = nc.dram_tensor("wo", [E, E], F32, kind="ExternalInput").ap()
    dbo = nc.dram_tensor("bo", [1, E], F32, kind="ExternalInput").ap()
    dw1 = nc.dram_tensor("w1", [FF, E], F32, kind="ExternalInput").ap()
    db1 = nc.dram_tensor("b1", [1, FF], F32, kind="ExternalInput").ap()
    dw2 = nc.dram_tensor("w2", [E, FF], F32, kind="ExternalInput").ap()
    db2 = nc.dram_tensor("b2", [1, E], F32, kind="ExternalInput").ap()
    dident = nc.dram_tensor("ident", [128, 128], BF16, kind="ExternalInput").ap()
    dmask = nc.dram_tensor("mask4", [128, 512], BF16, kind="ExternalInput").ap()
    dout = nc.dram_tensor("out", [S_LEN, E], F32, kind="ExternalOutput").ap()
    dO = nc.dram_tensor("oscr", [H, S_LEN, D], BF16, kind="Internal").ap()
    dW2T = nc.dram_tensor("w2tscr", [128, 32 * 1024], BF16, kind="Internal").ap()

    with ExitStack() as ctx:
        arena = ctx.enter_context(nc.sbuf_tensor("arena", [128, ARENA_COLS], BF16))
        arena2 = ctx.enter_context(nc.sbuf_tensor("arena2", [128, ARENA2_F32], F32))

        def bfv(off, cols, parts=128, p0=0):
            return arena[p0:p0 + parts, off:off + cols]

        def f32v(off, cols, parts=128, p0=0):
            return arena2[p0:p0 + parts, off // 2:off // 2 + cols]

        sbt = lambda n_, s_, d_=F32: ctx.enter_context(nc.sbuf_tensor(n_, s_, d_))
        ident = sbt("identb", [128, 128], BF16)
        mask4 = bfv(O_MSK, 512)
        onesb = sbt("onesb", [1, 128], BF16)
        onesf = sbt("onesf", [128, 1], F32)
        onesfr = sbt("onesfr", [1, 128], F32)
        b1r = bfv(R_ROWS, FF, parts=1, p0=0)
        bqr = bfv(R_ROWS, F3, parts=1, p0=32)
        onesb32 = bfv(R_ROWS + 3072, 128, parts=1, p0=32)
        bor = bfv(R_ROWS, E, parts=1, p0=64)
        b2r = bfv(R_ROWS + 1024, E, parts=1, p0=64)
        onesb64 = bfv(R_ROWS + 2048, 128, parts=1, p0=64)
        epsc = sbt("epsc", [128, 1], F32)
        halfp = sbt("halfp", [128, 1], F32)
        halfn = sbt("halfn", [128, 1], F32)
        m128c = sbt("m128c", [128, 1], F32)
        c_sum = sbt("c_sum", [128, 1], F32)
        c_ss = sbt("c_ss", [128, 1], F32)
        c_mean = sbt("c_mean", [128, 1], F32)
        c_var = sbt("c_var", [128, 1], F32)
        c_rstd = sbt("c_rstd", [128, 1], F32)
        c_nb = sbt("c_nb", [128, 1], F32)
        c_mx = sbt("c_mx", [128, 1], F32)
        c_mn = sbt("c_mn", [128, 1], F32)
        c_t1 = sbt("c_t1", [128, 1], F32)
        c_t2 = sbt("c_t2", [128, 1], F32)
        c_alpha = sbt("c_alpha", [128, 1], F32)
        c_sact = sbt("c_sact", [128, 1], F32)
        c_sinv = sbt("c_sinv", [128, 1], F32)
        c_ca = sbt("c_ca", [128, 1], F32)
        c_cb = sbt("c_cb", [128, 1], F32)
        c_rec = sbt("c_rec", [128, 4], F32)
        q_acc1 = sbt("q_acc1", [128, 1], F32)
        q_acc2 = sbt("q_acc2", [128, 1], F32)
        q_tmp = sbt("q_tmp", [128, 1], F32)
        q_s1 = sbt("q_s1", [128, 1], F32)
        q_s2 = sbt("q_s2", [128, 1], F32)
        q_m1 = sbt("q_m1", [128, 1], F32)
        q_m2 = sbt("q_m2", [128, 1], F32)
        q_sc = sbt("q_sc", [1, 2], F32)
        c_em = sbt("c_em", [128, 1], F32)
        c_ev = sbt("c_ev", [128, 1], F32)
        c_er = sbt("c_er", [128, 1], F32)
        c_enb = sbt("c_enb", [128, 1], F32)
        c_eca = sbt("c_eca", [128, 1], F32)
        c_es1 = sbt("c_es1", [128, 1], F32)
        c_es2 = sbt("c_es2", [128, 1], F32)
        junk = bfv(B_JNK, 512)
        c_ss2 = sbt("c_ss2", [128, 1], F32)
        c_h = sbt("c_h", [128, 8], F32)

        ps = [ctx.enter_context(nc.psum_tensor("ps%d" % i, [128, 512], F32))
              for i in range(8)]

        sems = {}
        for name in ["pDma0", "pDma1", "pC", "pCast", "pTr", "pTre", "pMean", "pMeanE", "pBias",
                     "pSgn", "pCmb", "pW2d",
                     "aX0", "aX1", "aLN", "aT", "aTT", "aTTe", "aQKV", "aQKVe", "aPK0", "aPK1", "aPK2", "aPK3",
                     "aQKT", "aQKTe", "aGR", "aEXP", "aMSK", "aAV", "aAVe",
                     "aOT", "aDIV", "aOW", "wW2",
                     "bO", "bX", "bTT", "bTTe", "bWo", "bYs", "bYa", "bZ",
                     "bZs", "bCA", "bCF", "bUT", "bUTe", "bW1", "bGel", "bHs",
                     "bVT", "bVTe", "bW2", "bOs", "bOa", "bOut", "bAdd", "bSq", "aSq", "vS", "sS", "sSG", "vFD"]:
            sems[name] = ctx.enter_context(nc.semaphore(name))
        SM = sems

        with nc.Block() as blk:

            @blk.sync
            def _(sync):
                ld = 0
                srcs = []
                for i in range(N_WQ):
                    srcs.append(dwq[128 * i:128 * (i + 1), :])
                for i in range(N_WO):
                    srcs.append(dwo[128 * i:128 * (i + 1), :])
                w1srcs = [dw1[128 * i:128 * (i + 1), :] for i in range(N_W1)]
                w2srcs = [dw2[128 * (i // 4):128 * (i // 4) + 128,
                              1024 * (i % 4):1024 * (i % 4 + 1)] for i in range(N_W2)]
                srcs += w1srcs + w2srcs
                for src in srcs:
                    if ld >= 2:
                        sync.wait_ge(SM["pCast"], ld - 1)
                    sync.dma_start(out=f32v(O_STAG + 2048 * (ld % 2), 1024),
                                   in_=src).then_inc(SM["pDma%d" % (ld % 2)], 16)
                    ld += 1
                for k, src in enumerate(w1srcs + w2srcs):
                    if k < 2:
                        sync.wait_ge(SM["pCast"], NPRE + NSUM)
                    else:
                        sync.wait_ge(SM["pSgn"], 2 * (k - 1))
                    sync.dma_start(out=f32v(O_STAG + 2048 * (ld % 2), 1024),
                                   in_=src).then_inc(SM["pDma%d" % (ld % 2)], 16)
                    ld += 1
                for i in range(NT):
                    if i >= 2:
                        sync.wait_ge(SM["aT"], 2 * (i - 1))
                    sync.dma_start(out=f32v(O_XF + 2048 * (i % 2), 1024),
                                   in_=dx[128 * i:128 * (i + 1), :]).then_inc(SM["aX%d" % (i % 2)], 16)
                sync.wait_ge(SM["aOW"], 256 * NT)
                sync.wait_ge(SM["aQKV"], 6 * NT)
                sync.wait_ge(SM["pW2d"], 16 * 8 * N_W2)
                for kt in range(32):
                    sync.dma_start(out=bfv(O_W2T + 1024 * kt, 1024),
                                   in_=dW2T[:, 1024 * kt:1024 * (kt + 1)]).then_inc(SM["wW2"], 16)
                o2flat = dO.rearrange("h s d -> (h s) d").rearrange("(r c) d -> r (c d)", c=16)
                for j in range(NT):
                    sync.wait_ge(SM["aOW"], 256 * NT)
                    if j >= 1:
                        sync.wait_ge(SM["bTT"], j)
                    sync.dma_start(out=bfv(B_OUT2, 1024),
                                   in_=o2flat[128 * j:128 * (j + 1), :]).then_inc(SM["bO"], 16)

            @blk.vector
            def _(v_raw):
                v = SEng(v_raw, SM["vS"])
                v.memset(onesb[:], 1.0)
                v.memset(onesb32, 1.0)
                v.memset(onesb64, 1.0)
                v.memset(onesf[:], 1.0)
                v.memset(onesfr[:], 1.0)
                v.memset(epsc[:], EPS)
                v.memset(halfp[:], 0.5)
                v.memset(halfn[:], -0.5)
                v.memset(m128c[:], -128.0)
                v.memset(q_acc1[:], 0.0)
                v.memset(q_acc2[:], 0.0)
                for g in range(16):
                    v.memset(bfv(O_VP + 65 * g + 64, 1), 1.0)
                for i in range(NPRE):
                    v.wait_ge(SM["pDma%d" % (i % 2)], 16 * (i // 2 + 1))
                    if i >= 1:
                        v.wait_ge(SM["pTr"], i)
                    v.tensor_copy(bfv(O_SP, 1024),
                                  f32v(O_STAG + 2048 * (i % 2), 1024)).then_inc(SM["pCast"], 1)
                for k in range(NSUM):
                    ld = NPRE + k
                    v.wait_ge(SM["pDma%d" % (ld % 2)], 16 * (ld // 2 + 1))
                    acc = q_acc1 if k < N_W1 else q_acc2
                    v.tensor_reduce(q_tmp[:], f32v(O_STAG + 2048 * (ld % 2), 1024),
                                    AX, ALU.add, apply_absolute_value=True)
                    v.tensor_add(acc[:], acc[:], q_tmp[:]).then_inc(SM["pCast"], 1)
                v.wait_ge(SM["pMeanE"], 2)
                for (col_s, col_m, nelem) in ((q_s1, q_m1, FF * E), (q_s2, q_m2, E * FF)):
                    v.tensor_scalar(col_m[:], col_s[:], 1.0 / nelem, None, ALU.mult)
                    v.tensor_scalar_max(col_m[:], col_m[:], 1e-5)
                    v.reciprocal(col_s[:], col_m[:])
                v.wait_ge(SM["pC"], 96)
                v.tensor_scalar(b1r, b1r, q_s1[0:1, 0:1], None, ALU.mult)
                v.tensor_scalar(b2r, b2r, q_s2[0:1, 0:1], None,
                                ALU.mult).then_inc(SM["pMean"], 1)
                for k in range(NQNT):
                    v.wait_ge(SM["pSgn"], 2 * (k + 1))
                    v.tensor_add(bfv(O_SP, 1024), bfv(O_SP, 1024), bfv(O_SM, 1024))
                    v.tensor_scalar(bfv(O_SP, 1024), bfv(O_SP, 1024), 0.5, None,
                                    ALU.mult).then_inc(SM["pCmb"], 1)

                for i in range(NT):
                    xf = f32v(O_XF + 2048 * (i % 2), 1024)
                    v.wait_ge(SM["aX%d" % (i % 2)], 16 * (i // 2 + 1))
                    v.wait_ge(SM["aSq"], 2 * (i + 1))
                    v.tensor_add(c_ss[:], c_h[:, 0:1], c_h[:, 1:2])
                    v.tensor_reduce(c_sum[:], xf, AX, ALU.add)
                    v.tensor_scalar(c_mean[:], c_sum[:], 1.0 / E, None, ALU.mult)
                    v.tensor_scalar(c_var[:], c_ss[:], 1.0 / E, None, ALU.mult)
                    v.tensor_mul(c_t1[:], c_mean[:], c_mean[:])
                    v.tensor_sub(c_var[:], c_var[:], c_t1[:]).then_inc(SM["aLN"], 1)
                    v.wait_ge(SM["aT"], 2 * i + 1)
                    v.reciprocal(c_rstd[:], c_rstd[:])
                    v.tensor_mul(c_nb[:], c_mean[:], c_rstd[:])
                    v.tensor_scalar(c_nb[:], c_nb[:], -1.0, None,
                                    ALU.mult).then_inc(SM["aLN"], 1)
                    for b in range(4):
                        v.wait_ge(SM["aEXP"], 4 * i + b + 1)
                        v.tensor_mul(bfv(O_S + 512 * b, 512), bfv(O_S + 512 * b, 512),
                                     mask4[:, :]).then_inc(SM["aMSK"], 1)
                    for b in range(4):
                        v.wait_ge(SM["aOT"], 4 * i + b + 1)
                        if i >= 1 and b == 0:
                            v.wait_ge(SM["aOW"], 256 * i)
                        pt3 = ps[b % 2][:, 0:260].rearrange("p (g c) -> p g c", g=4)
                        v.reciprocal(c_rec[:, 0:4].rearrange("p (g c) -> p g c", c=1),
                                     pt3[:, :, 64:65])
                        v.tensor_tensor(
                            bfv(O_OSB + 256 * b, 256).rearrange("p (g d) -> p g d", g=4),
                            pt3[:, :, 0:64],
                            c_rec[:, :, None].broadcast_to([128, 4, 64]),
                            ALU.mult).then_inc(SM["aDIV"], 1)

                casc_sg = [0]

                def epiV(jj):
                    v.wait_ge(SM["bW2"], 8 * (jj + 1))
                    v.wait_ge(SM["bSq"], 4 * jj + 4)
                    v.tensor_add(c_ev[:], c_es1[:], c_es2[:])
                    v.tensor_reduce(c_em[:], ps[6][:, :], AX, ALU.add)
                    v.tensor_reduce(c_t2[:], ps[7][:, :], AX, ALU.add)
                    v.tensor_add(c_em[:], c_em[:], c_t2[:])
                    v.tensor_scalar(c_em[:], c_em[:], 1.0 / E, None, ALU.mult)
                    v.tensor_scalar(c_ev[:], c_ev[:], 1.0 / E, None, ALU.mult)
                    v.tensor_mul(c_t2[:], c_em[:], c_em[:])
                    v.tensor_sub(c_ev[:], c_ev[:], c_t2[:])
                    v.tensor_mul(c_t2[:], q_m2[:], q_m2[:])
                    v.tensor_mul(c_ev[:], c_ev[:], c_t2[:]).then_inc(SM["bOs"], 1)
                    v.wait_ge(SM["bYa"], 7 * jj + 7)
                    v.reciprocal(c_er[:], c_er[:])
                    v.tensor_mul(c_eca[:], q_m2[:], c_er[:])
                    v.tensor_mul(c_enb[:], c_em[:], c_eca[:])
                    v.tensor_scalar(c_enb[:], c_enb[:], -1.0, None,
                                    ALU.mult).then_inc(SM["bOs"], 1)
                    v.wait_ge(SM["bOa"], jj + 1)
                    if jj >= 1:
                        v.wait_ge(SM["bAdd"], 16 * jj)
                    v.tensor_add(f32v(B_OST, 1024), f32v(B_Z, 1024),
                                 bfv(B_SGN, 1024)).then_inc(SM["bOut"], 1)

                for j in range(NT):
                    zv = f32v(B_Z, 1024)
                    if j >= 1:
                        epiV(j - 1)
                    v.wait_ge(SM["bWo"], 2 * j + 2)
                    v.wait_ge(SM["bSq"], 4 * j + 2)
                    v.tensor_add(c_ss[:], c_h[:, 0:1], c_h[:, 1:2])
                    v.tensor_reduce(c_sum[:], ps[0][:, :], AX, ALU.add)
                    v.tensor_reduce(c_t1[:], ps[1][:, :], AX, ALU.add)
                    v.tensor_add(c_sum[:], c_sum[:], c_t1[:])
                    v.tensor_scalar(c_mean[:], c_sum[:], 1.0 / E, None, ALU.mult)
                    v.tensor_scalar(c_var[:], c_ss[:], 1.0 / E, None, ALU.mult)
                    v.tensor_mul(c_t1[:], c_mean[:], c_mean[:])
                    v.tensor_sub(c_var[:], c_var[:], c_t1[:]).then_inc(SM["bYs"], 1)
                    v.wait_ge(SM["bYa"], 7 * j + 1)
                    v.reciprocal(c_rstd[:], c_rstd[:])
                    v.tensor_mul(c_nb[:], c_mean[:], c_rstd[:])
                    v.tensor_scalar(c_nb[:], c_nb[:], -1.0, None,
                                    ALU.mult).then_inc(SM["bYs"], 1)
                    v.wait_ge(SM["bYa"], 7 * j + 3)
                    v.wait_ge(SM["bX"], 16 * (j + 1))
                    v.tensor_add(zv, bfv(B_XB, 1024), bfv(B_SCR, 1024)).then_inc(SM["bZ"], 1)
                    v.scalar_tensor_tensor(junk[:, :], f32v(B_Z, 512), 1.0,
                                           f32v(B_Z, 512), ALU.mult, ALU.mult,
                                           accum_out=c_h[:, 2:3])
                    v.scalar_tensor_tensor(junk[:, :], f32v(B_Z + 1024, 512), 1.0,
                                           f32v(B_Z + 1024, 512), ALU.mult, ALU.mult,
                                           accum_out=c_h[:, 3:4])
                    v.tensor_add(c_ss[:], c_h[:, 2:3], c_h[:, 3:4])
                    v.tensor_reduce(c_sum[:], zv, AX, ALU.add)
                    v.tensor_reduce(c_mx[:], zv, AX, ALU.max)
                    v.tensor_reduce(c_mn[:], zv, AX, ALU.min)
                    v.tensor_scalar(c_mean[:], c_sum[:], 1.0 / E, None, ALU.mult)
                    v.tensor_scalar(c_var[:], c_ss[:], 1.0 / E, None, ALU.mult)
                    v.tensor_mul(c_t1[:], c_mean[:], c_mean[:])
                    v.tensor_sub(c_var[:], c_var[:], c_t1[:]).then_inc(SM["bZs"], 1)
                    v.wait_ge(SM["bYa"], 7 * j + 4)
                    v.reciprocal(c_rstd[:], c_rstd[:])
                    v.tensor_mul(c_t1[:], c_rstd[:], c_rstd[:])
                    v.tensor_mul(c_t2[:], c_var[:], c_t1[:]).then_inc(SM["bZs"], 1)
                    v.wait_ge(SM["bYa"], 7 * j + 5)
                    v.reciprocal(c_alpha[:], c_alpha[:])
                    v.tensor_sub(c_t1[:], c_mx[:], c_mean[:])
                    v.tensor_sub(c_t2[:], c_mean[:], c_mn[:])
                    v.tensor_max(c_t1[:], c_t1[:], c_t2[:])
                    v.tensor_mul(c_t1[:], c_t1[:], c_rstd[:])
                    v.tensor_mul(c_t1[:], c_t1[:], c_alpha[:])
                    v.tensor_scalar_max(c_t1[:], c_t1[:], 1e-5)
                    v.reciprocal(c_sact[:], c_t1[:])
                    v.tensor_scalar(c_sact[:], c_sact[:], 127.0, None, ALU.mult)
                    v.reciprocal(c_sinv[:], c_sact[:])
                    v.tensor_mul(c_ca[:], c_rstd[:], c_alpha[:])
                    v.tensor_mul(c_ca[:], c_ca[:], c_sact[:])
                    v.tensor_mul(c_cb[:], c_mean[:], c_ca[:])
                    v.tensor_scalar(c_cb[:], c_cb[:], -1.0, 128.5, ALU.mult,
                                    ALU.add).then_inc(SM["bZs"], 1)


                    def cascade_pair(chs, cids, int_sems, int_vals):
                        y0s = [f32v(B_Y0 + 1024 * (c % 2), 512) for c in chs]
                        rrs = [f32v(B_RES + 1024 * (c % 2), 512) for c in chs]
                        sgs = [bfv(B_SGN + 512 * (c % 2), 512) for c in chs]
                        for kb in range(7, -1, -1):
                            for ii in range(len(chs)):
                                casc_sg[0] += 1
                                v.wait_ge(SM["sSG"], casc_sg[0])
                                src = y0s[ii] if kb == 7 else rrs[ii]
                                v.scalar_tensor_tensor(
                                    rrs[ii], sgs[ii], -float(2 ** (kb - 1)), src,
                                    ALU.mult, ALU.add, _nowait=True).then_inc(SM["vFD"], 1)
                        for ii in range(len(chs)):
                            scr = f32v(F_SCR, 512)
                            v.scalar_tensor_tensor(scr, rrs[ii], -1.0, y0s[ii],
                                                   ALU.mult, ALU.add)
                            v.tensor_scalar(scr, scr, -0.5, None, ALU.add)
                            if int_vals[ii] > 0:
                                v.wait_ge(SM[int_sems[ii]], int_vals[ii])
                            v.tensor_scalar(bfv(B_INT, 512), scr, c_sinv[:, 0:1],
                                            None, ALU.mult).then_inc(SM["bCF"], 1)

                    v.wait_ge(SM["bVT"], 8 * j)
                    cascade_pair([0, 1], [10 * j, 10 * j + 1],
                                 ["bUT", "bUT"], [2 * j, 2 * j + 1])
                    v.wait_ge(SM["bGel"], 8 * (j + 1))
                    for c_ in range(8):
                        hgc = bfv(B_HG + 512 * c_, 512)
                        v.scalar_tensor_tensor(junk[:, :], hgc, 1.0, hgc,
                                               ALU.mult, ALU.mult,
                                               accum_out=c_h[:, c_:c_ + 1])
                    v.tensor_reduce(c_ss[:], c_h[:, 0:8], AX, ALU.add)
                    v.tensor_reduce(c_mx[:], bfv(B_HG, 4096), AX, ALU.max,
                                    apply_absolute_value=True)
                    v.tensor_scalar(c_var[:], c_ss[:], 1.0 / FF, None,
                                    ALU.mult).then_inc(SM["bHs"], 1)
                    v.wait_ge(SM["bYa"], 7 * j + 6)
                    v.reciprocal(c_alpha[:], c_alpha[:])
                    v.tensor_mul(c_t1[:], c_mx[:], c_alpha[:])
                    v.tensor_scalar_max(c_t1[:], c_t1[:], 1e-5)
                    v.reciprocal(c_sact[:], c_t1[:])
                    v.tensor_scalar(c_sact[:], c_sact[:], 127.0, None, ALU.mult)
                    v.reciprocal(c_sinv[:], c_sact[:])
                    v.tensor_mul(c_ca[:], c_alpha[:], c_sact[:])
                    v.memset(c_cb[:], 128.5).then_inc(SM["bHs"], 1)
                    for pp_ in range(4):
                        ca, cb = 2 * pp_, 2 * pp_ + 1
                        cascade_pair([ca, cb], [10 * j + 2 + ca, 10 * j + 2 + cb],
                                     ["bVT", "bVT"], [8 * j + ca, 8 * j + cb])


                epiV(NT - 1)

            @blk.scalar
            def _(s_raw):
                s = SEng(s_raw, SM["sS"])
                for i in range(NPRE):
                    s.wait_ge(SM["pTr"], i + 1)
                    for half in range(2):
                        for q in range(4):
                            et = half * 4 + q
                            if i < N_WQ:
                                dst = bfv(O_WQT + et * F3 + i * 128, 128)
                            else:
                                dst = bfv(O_WOT + et * E + (i - N_WQ) * 128, 128)
                            ins = s.activation(dst, ps[half][:, 128 * q:128 * (q + 1)],
                                               ACT.Copy)
                    ins.then_inc(SM["pTre"], 1)
                s.wait_ge(SM["pTr"], NPRE + 1)
                s.activation(q_sc[0:1, 0:2], ps[4][0:1, 0:2], ACT.Copy).then_inc(SM["pMeanE"], 1)
                s.wait_ge(SM["pTr"], NPRE + 2)
                s.activation(q_s1[:, 0:1], ps[5][:, 0:1], ACT.Copy)
                s.activation(q_s2[:, 0:1], ps[5][:, 1:2], ACT.Copy).then_inc(SM["pMeanE"], 1)
                for k in range(NQNT):
                    ld = NPRE + NSUM + k
                    s.wait_ge(SM["pDma%d" % (ld % 2)], 16 * (ld // 2 + 1))
                    s.wait_ge(SM["pMean"], 1)
                    if k == 0:
                        s.wait_ge(SM["pTr"], NPRE)
                    else:
                        s.wait_ge(SM["pCmb"], k)
                        s.wait_ge(SM["pTr"], NPRE + 2 + k)
                    scol = q_s1 if k < N_W1 else q_s2
                    src = f32v(O_STAG + 2048 * (ld % 2), 1024)
                    s.activation(bfv(O_SP, 1024), src, ACT.Sign, bias=halfn[:, 0:1],
                                 scale=scol[:, 0:1])
                    s.activation(bfv(O_SM, 1024), src, ACT.Sign, bias=halfp[:, 0:1],
                                 scale=scol[:, 0:1]).then_inc(SM["pSgn"], 2)
                    s.wait_ge(SM["pTr"], NPRE + 2 + k + 1)
                    if k >= N_W1:
                        s.wait_ge(SM["pW2d"], 16 * 8 * (k - N_W1))
                    for half in range(2):
                        for q in range(4):
                            et = half * 4 + q
                            if k < N_W1:
                                dst = bfv(O_W1T + et * FF + k * 128, 128)
                            else:
                                dst = bfv(O_W2S + et * 128, 128)
                            ins = s.activation(dst, ps[half][:, 128 * q:128 * (q + 1)],
                                               ACT.Copy)
                    ins.then_inc(SM["pTre"], 1)

                for i in range(NT):
                    xf = f32v(O_XF + 2048 * (i % 2), 1024)
                    s.wait_ge(SM["aX%d" % (i % 2)], 16 * (i // 2 + 1))
                    if i >= 1:
                        s.wait_ge(SM["aLN"], 2 * i - 1)
                    s.activation(bfv(O_JNKA, 512), xf[:, 0:512], ACT.Square,
                                 accum_out=c_h[:, 0:1]).then_inc(SM["aSq"], 1)
                    s.activation(bfv(O_JNKA, 512), xf[:, 512:1024], ACT.Square,
                                 accum_out=c_h[:, 1:2]).then_inc(SM["aSq"], 1)
                    s.wait_ge(SM["aLN"], 2 * i + 1)
                    s.activation(c_rstd[:], c_var[:], ACT.Sqrt,
                                 bias=epsc[:, 0:1]).then_inc(SM["aT"], 1)
                    s.wait_ge(SM["aLN"], 2 * i + 2)
                    if i >= 1:
                        s.wait_ge(SM["aTT"], i)
                    s.activation(bfv(O_TNAT, 1024), xf, ACT.Identity,
                                 bias=c_nb[:, 0:1], scale=c_rstd[:, 0:1]).then_inc(SM["aT"], 1)
                    s.wait_ge(SM["aTT"], i + 1)
                    s.activation(bfv(O_TT, 512), ps[2][:, :], ACT.Copy)
                    s.activation(bfv(O_TT + 512, 512), ps[3][:, :],
                                 ACT.Copy).then_inc(SM["aTTe"], 1)
                    for ch in range(6):
                        s.wait_ge(SM["aQKV"], 6 * i + ch + 1)
                        s.activation(bfv(O_QKV + 512 * ch, 512), ps[2 + ch % 2][:, :],
                                     ACT.Copy).then_inc(SM["aQKVe"], 1)
                    for b in range(8):
                        s.wait_ge(SM["aQKT"], 8 * i + b + 1)
                        if b == 0 and i >= 1:
                            s.wait_ge(SM["aGR"], 4 * i)
                        dsto = O_QT if b < 4 else O_KT
                        s.activation(bfv(dsto + 512 * (b % 4), 512, parts=64),
                                     ps[b % 2][0:64, :], ACT.Copy).then_inc(SM["aQKTe"], 1)
                    for b in range(4):
                        s.wait_ge(SM["aGR"], 4 * i + b + 1)
                        if b == 0 and i >= 1:
                            s.wait_ge(SM["aAV"], 4 * i)
                        s.activation(bfv(O_S + 512 * b, 512), ps[4 + b % 2][:, :],
                                     ACT.Exp, scale=0.03125).then_inc(SM["aEXP"], 1)
                    for b in range(4):
                        s.wait_ge(SM["aAV"], 4 * i + b + 1)
                        if b == 0 and i >= 1:
                            s.wait_ge(SM["aOT"], 4 * i)
                        s.activation(bfv(O_AV + 512 * b, 512, parts=65),
                                     ps[6 + b % 2][0:65, :], ACT.Copy).then_inc(SM["aAVe"], 1)

                sg_sched = [0]

                def epiS(jj):
                    s.wait_ge(SM["bW2"], 8 * (jj + 1))
                    s.activation(junk[:, :], ps[6][:, :], ACT.Square,
                                 accum_out=c_es1[:, 0:1]).then_inc(SM["bSq"], 1)
                    s.activation(junk[:, :], ps[7][:, :], ACT.Square,
                                 accum_out=c_es2[:, 0:1]).then_inc(SM["bSq"], 1)
                    s.wait_ge(SM["bOs"], 2 * jj + 1)
                    s.activation(c_er[:], c_ev[:], ACT.Sqrt,
                                 bias=epsc[:, 0:1]).then_inc(SM["bYa"], 1)
                    s.wait_ge(SM["bOs"], 2 * jj + 2)
                    s.activation(bfv(B_SGN, 512), ps[6][:, :], ACT.Identity,
                                 bias=c_enb[:, 0:1], scale=c_eca[:, 0:1])
                    s.activation(bfv(B_SGN + 512, 512), ps[7][:, :], ACT.Identity,
                                 bias=c_enb[:, 0:1], scale=c_eca[:, 0:1]).then_inc(SM["bOa"], 1)

                for j in range(NT):
                    if j >= 1:
                        epiS(j - 1)
                    s.wait_ge(SM["bTT"], j + 1)
                    if j >= 1:
                        s.wait_ge(SM["bWo"], 2 * j)
                    s.activation(bfv(B_O2T, 512), ps[0][:, :], ACT.Copy)
                    s.activation(bfv(B_O2T + 512, 512), ps[1][:, :],
                                 ACT.Copy).then_inc(SM["bTTe"], 1)
                    s.wait_ge(SM["bWo"], 2 * j + 2)
                    if j >= 1:
                        s.wait_ge(SM["bHs"], 2 * j - 1)
                    s.activation(junk[:, :], ps[0][:, :], ACT.Square,
                                 accum_out=c_h[:, 0:1]).then_inc(SM["bSq"], 1)
                    s.activation(junk[:, :], ps[1][:, :], ACT.Square,
                                 accum_out=c_h[:, 1:2]).then_inc(SM["bSq"], 1)
                    s.wait_ge(SM["bYs"], 2 * j + 1)
                    s.activation(c_rstd[:], c_var[:], ACT.Sqrt,
                                 bias=epsc[:, 0:1]).then_inc(SM["bYa"], 1)
                    s.wait_ge(SM["bYs"], 2 * j + 2)
                    s.activation(bfv(B_SCR, 512), ps[0][:, :], ACT.Identity,
                                 bias=c_nb[:, 0:1], scale=c_rstd[:, 0:1]).then_inc(SM["bYa"], 1)
                    s.activation(bfv(B_SCR + 512, 512), ps[1][:, :], ACT.Identity,
                                 bias=c_nb[:, 0:1], scale=c_rstd[:, 0:1]).then_inc(SM["bYa"], 1)
                    s.wait_ge(SM["bZs"], 3 * j + 1)
                    s.activation(c_rstd[:], c_var[:], ACT.Sqrt,
                                 bias=epsc[:, 0:1]).then_inc(SM["bYa"], 1)
                    s.wait_ge(SM["bZs"], 3 * j + 2)
                    s.activation(c_alpha[:], c_t2[:], ACT.Sqrt,
                                 bias=epsc[:, 0:1]).then_inc(SM["bYa"], 1)
                    zv = f32v(B_Z, 1024)

                    def affine_pair(srcs, cids):
                        for src, cid in zip(srcs, cids):
                            if cid >= 2:
                                s.wait_ge(SM["bCF"], cid - 1)
                            s.activation(f32v(B_Y0 + 1024 * (cid % 2), 512),
                                         src, ACT.Identity,
                                         bias=c_cb[:, 0:1], scale=c_ca[:, 0:1]).then_inc(SM["bCA"], 1)

                    def sign_pair(chs):
                        sgs = [bfv(B_SGN + 512 * (c % 2), 512) for c in chs]
                        rrs = [f32v(B_RES + 1024 * (c % 2), 512) for c in chs]
                        y0s = [f32v(B_Y0 + 1024 * (c % 2), 512) for c in chs]
                        for kb in range(7, -1, -1):
                            for ii in range(len(chs)):
                                if kb == 7:
                                    s.wait_ge(SM["vFD"], sg_sched[0])
                                    src = y0s[ii]
                                else:
                                    sg_sched[0] += 1
                                    s.wait_ge(SM["vFD"], sg_sched[0])
                                    src = rrs[ii]
                                s.activation(sgs[ii], src, ACT.Sign,
                                             bias=m128c[:, 0:1], _nowait=(kb < 7)).then_inc(SM["sSG"], 1)
                        sg_sched[0] += len(chs)

                    s.wait_ge(SM["bZs"], 3 * j + 3)
                    if j >= 1:
                        s.wait_ge(SM["bOut"], j)
                    affine_pair([zv[:, 0:512], zv[:, 512:1024]], [10 * j, 10 * j + 1])
                    sign_pair([0, 1])
                    for cch in range(2):
                        s.wait_ge(SM["bUT"], 2 * j + cch + 1)
                        if j >= 1 and cch == 0:
                            s.wait_ge(SM["bW1"], 8 * j)
                        s.activation(bfv(B_UQT + 512 * cch, 512), ps[2 + cch][:, :],
                                     ACT.Copy).then_inc(SM["bUTe"], 1)
                    for ch in range(8):
                        s.wait_ge(SM["bW1"], 8 * j + ch + 1)
                        if j >= 1 and ch == 0:
                            s.wait_ge(SM["bCF"], 10 * j)
                        s.activation(bfv(B_HG + 512 * ch, 512), ps[4 + ch % 2][:, :],
                                     ACT.Gelu, scale=q_m1[:, 0:1]).then_inc(SM["bGel"], 1)
                    s.wait_ge(SM["bHs"], 2 * j + 1)
                    s.activation(c_alpha[:], c_var[:], ACT.Sqrt,
                                 bias=epsc[:, 0:1]).then_inc(SM["bYa"], 1)
                    def vq_evac(cch):
                        s.wait_ge(SM["bVT"], 8 * j + cch + 1)
                        if j == 0 and cch == 0:
                            s.wait_ge(SM["pW2d"], 16 * 8 * N_W2)
                        if cch >= 2:
                            s.wait_ge(SM["bW2"], 8 * j + cch - 1)
                        elif j >= 1:
                            s.wait_ge(SM["bW2"], 8 * j)
                        s.activation(bfv(B_VQT + 512 * (cch % 2), 512), ps[2 + cch % 2][:, :],
                                     ACT.Copy).then_inc(SM["bVTe"], 1)

                    s.wait_ge(SM["bHs"], 2 * j + 2)
                    for pp_ in range(4):
                        ca, cb = 2 * pp_, 2 * pp_ + 1
                        affine_pair([bfv(B_HG + 512 * ca, 512), bfv(B_HG + 512 * cb, 512)],
                                    [10 * j + 2 + ca, 10 * j + 2 + cb])
                        sign_pair([ca, cb])
                        if pp_ >= 1:
                            vq_evac(2 * pp_ - 2)
                            vq_evac(2 * pp_ - 1)
                    vq_evac(6)
                    vq_evac(7)


                epiS(NT - 1)

            @blk.tensor
            def _(t):
                for i in range(NPRE):
                    t.wait_ge(SM["pCast"], i + 1)
                    if i == 0:
                        t.wait_ge(SM["pC"], 96)
                    if i >= 1:
                        t.wait_ge(SM["pTre"], i)
                    for half in range(2):
                        for q in range(4):
                            et = half * 4 + q
                            ins = t.matmul(ps[half][:, 128 * q:128 * (q + 1)],
                                           bfv(O_SP + 128 * et, 128), ident[:, :],
                                           start=True, stop=True)
                    ins.then_inc(SM["pTr"], 1)
                t.wait_ge(SM["pCast"], NPRE + NSUM)
                t.matmul(ps[4][0:1, 0:1], q_acc1[:, 0:1], onesf[:, 0:1], start=True, stop=True)
                t.matmul(ps[4][0:1, 1:2], q_acc2[:, 0:1], onesf[:, 0:1],
                         start=True, stop=True).then_inc(SM["pTr"], 1)
                t.wait_ge(SM["pMeanE"], 1)
                t.matmul(ps[5][:, 0:1], onesfr[0:1, :], q_sc[0:1, 0:1], start=True, stop=True)
                t.matmul(ps[5][:, 1:2], onesfr[0:1, :], q_sc[0:1, 1:2],
                         start=True, stop=True).then_inc(SM["pTr"], 1)
                for k in range(NQNT):
                    t.wait_ge(SM["pCmb"], k + 1)
                    t.wait_ge(SM["pTre"], NPRE + k)
                    for half in range(2):
                        for q in range(4):
                            et = half * 4 + q
                            ins = t.matmul(ps[half][:, 128 * q:128 * (q + 1)],
                                           bfv(O_SP + 128 * et, 128), ident[:, :],
                                           start=True, stop=True)
                    ins.then_inc(SM["pTr"], 1)

                for i in range(NT):
                    t.wait_ge(SM["aT"], 2 * i + 2)
                    if i == 0:
                        t.wait_ge(SM["pTre"], NPRE + NQNT)
                    else:
                        t.wait_ge(SM["aQKVe"], 6 * i)
                    for q in range(8):
                        ins = t.matmul(ps[2 + q // 4][:, 128 * (q % 4):128 * (q % 4 + 1)],
                                       bfv(O_TNAT + 128 * q, 128), ident[:, :],
                                       start=True, stop=True)
                    ins.then_inc(SM["aTT"], 1)
                    for ch in range(6):
                        gc = 6 * i + ch
                        t.wait_ge(SM["aTTe"], i + 1)
                        if gc >= 2:
                            t.wait_ge(SM["aQKVe"], gc - 1)
                        if i >= 1 and ch == 0:
                            for qq in range(4):
                                t.wait_ge(SM["aPK%d" % qq], 192 * i)
                        bank = 2 + ch % 2
                        t.matmul(ps[bank][:, :], onesb32[0:1, :],
                                 bqr[0:1, 512 * ch:512 * (ch + 1)], start=True, stop=False)
                        for et in range(8):
                            ins = t.matmul(ps[bank][:, :], bfv(O_TT + 128 * et, 128),
                                           bfv(O_WQT + et * F3 + 512 * ch, 512),
                                           start=False, stop=(et == 7))
                        ins.then_inc(SM["aQKV"], 1)
                    for b in range(8):
                        po = O_QP if b < 4 else O_KP
                        t.wait_ge(SM["aPK%d" % (b % 4)], 192 * (i + 1))
                        if b < 2 and i >= 1:
                            t.wait_ge(SM["aDIV"], 4 * i)
                        if 8 * i + b >= 2:
                            t.wait_ge(SM["aQKTe"], 8 * i + b - 1)
                        for q in range(4):
                            g = 4 * (b % 4) + q
                            ins = t.matmul(ps[b % 2][0:64, 128 * q:128 * (q + 1)],
                                           bfv(po + 64 * g, 64), ident[:, :],
                                           start=True, stop=True)
                        ins.then_inc(SM["aQKT"], 1)
                    for b in range(4):
                        t.wait_ge(SM["aQKTe"], 8 * i + 5 + b)
                        if 4 * i + b >= 2:
                            t.wait_ge(SM["aEXP"], 4 * i + b - 1)
                        for q in range(4):
                            g = 4 * b + q
                            ins = t.matmul(ps[4 + b % 2][:, 128 * q:128 * (q + 1)],
                                           bfv(O_KT + 128 * g, 128, parts=64),
                                           bfv(O_QT + 128 * g, 128, parts=64),
                                           start=True, stop=True)
                        ins.then_inc(SM["aGR"], 1)
                    for b in range(4):
                        t.wait_ge(SM["aMSK"], 4 * i + b + 1)
                        t.wait_ge(SM["aPK%d" % b], 192 * (i + 1))
                        if 4 * i + b >= 2:
                            t.wait_ge(SM["aAVe"], 4 * i + b - 1)
                        for q in range(4):
                            g = 4 * b + q
                            ins = t.matmul(ps[6 + b % 2][0:65, 128 * q:128 * (q + 1)],
                                           bfv(O_VP + 65 * g, 65),
                                           bfv(O_S + 128 * g, 128),
                                           start=True, stop=True)
                        ins.then_inc(SM["aAV"], 1)
                    for b in range(4):
                        t.wait_ge(SM["aAVe"], 4 * i + b + 1)
                        if b < 2:
                            t.wait_ge(SM["aQKTe"], 8 * i + 8)
                        if 4 * i + b >= 2:
                            t.wait_ge(SM["aDIV"], 4 * i + b - 1)
                        for q in range(4):
                            g = 4 * b + q
                            ins = t.matmul(ps[b % 2][:, 65 * q:65 * (q + 1)],
                                           bfv(O_AV + 128 * g, 128, parts=65),
                                           ident[0:65, 0:65],
                                           start=True, stop=True)
                        ins.then_inc(SM["aOT"], 1)

                for j in range(NT):
                    t.wait_ge(SM["bO"], 16 * (j + 1))
                    if j >= 1:
                        t.wait_ge(SM["bTTe"], j)
                        t.wait_ge(SM["bYa"], 7 * (j - 1) + 3)
                        t.wait_ge(SM["bYs"], 2 * j)
                    for q in range(8):
                        ins = t.matmul(ps[q // 4][:, 128 * (q % 4):128 * (q % 4 + 1)],
                                       bfv(B_OUT2 + 128 * q, 128), ident[:, :],
                                       start=True, stop=True)
                    ins.then_inc(SM["bTT"], 1)
                    t.wait_ge(SM["bTTe"], j + 1)
                    if j >= 1:
                        t.wait_ge(SM["bZ"], j)
                    for ch in range(2):
                        t.matmul(ps[ch][:, :], onesb64[0:1, :],
                                 bor[0:1, 512 * ch:512 * (ch + 1)], start=True, stop=False)
                        for et in range(8):
                            ins = t.matmul(ps[ch][:, :], bfv(B_O2T + 128 * et, 128),
                                           bfv(O_WOT + et * E + 512 * ch, 512),
                                           start=False, stop=(et == 7))
                        ins.then_inc(SM["bWo"], 1)
                    for cch in range(2):
                        t.wait_ge(SM["bCF"], 10 * j + cch + 1)
                        if j >= 1 and cch == 0:
                            t.wait_ge(SM["bVTe"], 8 * j)
                        for q in range(4):
                            ins = t.matmul(ps[2 + cch][:, 128 * q:128 * (q + 1)],
                                           bfv(B_INT + 128 * q, 128), ident[:, :],
                                           start=True, stop=True)
                        ins.then_inc(SM["bUT"], 1)
                    t.wait_ge(SM["bUTe"], 2 * (j + 1))
                    for ch in range(8):
                        if 8 * j + ch >= 2:
                            t.wait_ge(SM["bGel"], 8 * j + ch - 1)
                        bank = 4 + ch % 2
                        t.matmul(ps[bank][:, :], onesb[0:1, :],
                                 b1r[0:1, 512 * ch:512 * (ch + 1)], start=True, stop=False)
                        for et in range(8):
                            ins = t.matmul(ps[bank][:, :], bfv(B_UQT + 128 * et, 128),
                                           bfv(O_W1T + et * FF + 512 * ch, 512),
                                           start=False, stop=(et == 7))
                        ins.then_inc(SM["bW1"], 1)
                    t.wait_ge(SM["bOa"], j)
                    t.wait_ge(SM["bOut"], j)
                    for ch in range(2):
                        t.matmul(ps[6 + ch][:, :], onesb64[0:1, :],
                                 b2r[0:1, 512 * ch:512 * (ch + 1)], start=True, stop=False)
                    for cch in range(8):
                        t.wait_ge(SM["bCF"], 10 * j + 2 + cch + 1)
                        if 8 * j + cch >= 2:
                            t.wait_ge(SM["bVTe"], 8 * j + cch - 1)
                        for q in range(4):
                            ins = t.matmul(ps[2 + cch % 2][:, 128 * q:128 * (q + 1)],
                                           bfv(B_INT + 128 * q, 128), ident[:, :],
                                           start=True, stop=True)
                        ins.then_inc(SM["bVT"], 1)
                        t.wait_ge(SM["bVTe"], 8 * j + cch + 1)
                        if j == 0 and cch == 0:
                            t.wait_ge(SM["wW2"], 512)
                        for qi in range(4):
                            kt = 4 * cch + qi
                            for ch in range(2):
                                ins = t.matmul(ps[6 + ch][:, :],
                                               bfv(B_VQT + 512 * (cch % 2) + 128 * qi, 128),
                                               bfv(O_W2T + 1024 * kt + 512 * ch, 512),
                                               start=False, stop=(cch == 7 and qi == 3))
                        ins.then_inc(SM["bW2"], 1)

            @blk.gpsimd
            def _(g):
                g.dma_start(out=ident[:, :], in_=dident).then_inc(SM["pC"], 16)
                g.dma_start(out=mask4[:, :], in_=dmask).then_inc(SM["pC"], 16)
                g.dma_start(out=bqr[:, :], in_=dbq).then_inc(SM["pC"], 16)
                g.dma_start(out=bor[:, :], in_=dbo).then_inc(SM["pC"], 16)
                g.dma_start(out=b1r, in_=db1).then_inc(SM["pC"], 16)
                g.dma_start(out=b2r, in_=db2).then_inc(SM["pC"], 16)
                for k in range(N_W2):
                    g.wait_ge(SM["pTre"], NPRE + N_W1 + k + 1)
                    eb, cc = k // 4, k % 4
                    for et in range(8):
                        kt = cc * 8 + et
                        g.dma_start(out=dW2T[:, 1024 * kt + 128 * eb:1024 * kt + 128 * (eb + 1)],
                                    in_=bfv(O_W2S + 128 * et, 128)).then_inc(SM["pW2d"], 16)
                for i in range(NT):
                    g.wait_ge(SM["aQKVe"], 6 * (i + 1))
                    if i >= 1:
                        g.wait_ge(SM["aQKT"], 8 * i)
                        g.wait_ge(SM["aAV"], 4 * i)
                    for grp in range(16):
                        r0 = 8 * grp
                        pq = SM["aPK%d" % (grp // 4)]
                        g.dma_start(out=bfv(O_QP + 64 * grp, 64),
                                    in_=bfv(O_QKV, 1024, parts=8, p0=r0)).then_inc(pq, 16)
                        g.dma_start(out=bfv(O_KP + 64 * grp, 64),
                                    in_=bfv(O_QKV + 1024, 1024, parts=8, p0=r0)).then_inc(pq, 16)
                        g.dma_start(out=bfv(O_VP + 65 * grp, 64),
                                    in_=bfv(O_QKV + 2048, 1024, parts=8, p0=r0)).then_inc(pq, 16)
                    for b in range(4):
                        g.wait_ge(SM["aDIV"], 4 * i + b + 1)
                        for q in range(4):
                            grp = 4 * b + q
                            p0 = 128 * i + 8 * grp
                            g.dma_start(out=dO[:, p0:p0 + 8, :].rearrange("h p d -> p h d"),
                                        in_=bfv(O_OSB + 256 * b + 64 * q, 64)).then_inc(SM["aOW"], 16)
                for j in range(NT):
                    g.wait_ge(SM["aOW"], 256 * NT)
                    if j >= 1:
                        g.wait_ge(SM["bCF"], 10 * j)
                    g.dma_start(out=bfv(B_XB, 1024),
                                in_=dx[128 * j:128 * (j + 1), :]).then_inc(SM["bX"], 16)
                    if j >= 1:
                        g.wait_ge(SM["bOut"], j)
                        g.dma_start(out=dout[128 * (j - 1):128 * j, :],
                                    in_=f32v(B_OST, 1024)).then_inc(SM["bAdd"], 16)
                g.wait_ge(SM["bOut"], NT)
                g.dma_start(out=dout[128 * (NT - 1):128 * NT, :],
                            in_=f32v(B_OST, 1024)).then_inc(SM["bAdd"], 16)
                g.wait_ge(SM["bAdd"], 16 * NT)

    return nc


def _consts():
    m = np.zeros((128, 512), dtype=ml_dtypes.bfloat16)
    for rep in range(4):
        for p in range(8):
            m[16 * p:16 * (p + 1), 128 * rep + 16 * p:128 * rep + 16 * (p + 1)] = 1
    return np.eye(128, dtype=ml_dtypes.bfloat16), m


def kernel(L0_attn_x, L0_attn_Wqkv, L0_attn_bqkv, L0_attn_Wo, L0_attn_bo,
           L0_ffn_w1, L0_ffn_b1, L0_ffn_w2, L0_ffn_b2):
    x = np.ascontiguousarray(np.asarray(L0_attn_x, dtype=np.float32))
    ident, mask = _consts()
    nc = build_program()
    base = dict(
        wq=np.ascontiguousarray(np.asarray(L0_attn_Wqkv, np.float32)),
        bq=np.ascontiguousarray(np.asarray(L0_attn_bqkv, np.float32)).reshape(1, -1),
        wo=np.ascontiguousarray(np.asarray(L0_attn_Wo, np.float32)),
        bo=np.ascontiguousarray(np.asarray(L0_attn_bo, np.float32)).reshape(1, -1),
        w1=np.ascontiguousarray(np.asarray(L0_ffn_w1, np.float32)),
        b1=np.ascontiguousarray(np.asarray(L0_ffn_b1, np.float32)).reshape(1, -1),
        w2=np.ascontiguousarray(np.asarray(L0_ffn_w2, np.float32)),
        b2=np.ascontiguousarray(np.asarray(L0_ffn_b2, np.float32)).reshape(1, -1),
        ident=ident, mask4=mask,
    )
    in_maps = [dict(base, x=np.ascontiguousarray(x[c])) for c in range(8)]
    res = run_bass_kernel_spmd(nc, in_maps, core_ids=list(range(8)))
    out = np.stack([np.asarray(r["out"], dtype=np.float32) for r in res.results], axis=0)
    return out


# version 11
# speedup vs baseline: 1.0884x; 1.0884x over previous
import numpy as np
import ml_dtypes
from contextlib import ExitStack

import concourse.bass as bass
import concourse.mybir as mybir
from concourse.bass_utils import run_bass_kernel_spmd

F32 = mybir.dt.float32
BF16 = mybir.dt.bfloat16


class _PIns:
    def __init__(self, owner, ins):
        self._o = owner
        self._ins = ins

    def then_inc(self, sem, n):
        o = self._o
        key = id(sem)
        o.counts[key] = o.counts.get(key, 0) + n
        o.pend = None
        o.last = (sem, o.counts[key])
        self._ins.then_inc(sem, n)
        return self._ins


class SEng:
    def __init__(self, e, selfsem):
        self.e = e
        self.selfsem = selfsem
        self.counts = {}
        self.pend = None
        self.last = None

    def _flush(self):
        if self.pend is not None:
            key = id(self.selfsem)
            self.counts[key] = self.counts.get(key, 0) + 1
            self.pend.then_inc(self.selfsem, 1)
            self.last = (self.selfsem, self.counts[key])
            self.pend = None

    def wait_ge(self, sem, val):
        self._flush()
        self.e.wait_ge(sem, val)

    def __getattr__(self, name):
        fn = getattr(self.e, name)

        def wrapped(*a, _nowait=False, **k):
            self._flush()
            if self.last is not None and not _nowait:
                self.e.wait_ge(*self.last)
            ins = fn(*a, **k)
            self.pend = ins
            return _PIns(self, ins)

        return wrapped
AX = mybir.AxisListType.X
ALU = mybir.AluOpType
ACT = mybir.ActivationFunctionType

S_LEN, E, H, D, F3, FF = 8192, 1024, 16, 64, 3072, 4096
NT = 64
EPS = 1e-5
N_WQ, N_WO, N_W1, N_W2 = 24, 8, 32, 32
NPRE = N_WQ + N_WO
NSUM = N_W1 + N_W2
NQNT = N_W1 + N_W2

O_W1T = 0
O_WQT = 32768
O_TT = O_WQT + 24576
O_QKV = O_TT + 1024
O_W2T = 32768
O_QP = 65536
O_KP = O_QP + 1024
O_VP = O_KP + 1024
O_QT = O_VP + 1040
O_KT = O_QT + 2048
O_S = O_KT + 2048
O_AV = O_S + 2048
O_OSB = O_AV + 2048
O_TNAT = O_OSB + 1024
B_HG = 65536
B_OUT2 = B_HG + 4096
B_O2T = B_OUT2 + 1024
B_UQT = B_O2T + 1024
B_VQT = B_UQT + 1024
B_JNK = B_VQT + 2048
B_SGN = B_JNK + 512
O_SP = 78912
O_SM = O_SP + 1024
O_W2S = O_SM + 1024
O_MSK = O_W2S + 1024
O_JNKA = O_MSK + 512
B_XB = 78912
B_INT = 79936
B_SCR = 80960
O_WOT = 83520
R_ROWS = O_WOT + 8192
ARENA_COLS = R_ROWS + 4096
O_XF = 0
O_STAG = 4096
B_Z = 0
B_OST = 2048
B_Y0 = 4096
B_RES = 6144
F_SCR = 8192
ARENA2_F32 = (F_SCR + 1024) // 2


def build_program():
    nc = bass.Bass("TRN2", target_bir_lowering=False)
    dx = nc.dram_tensor("x", [S_LEN, E], F32, kind="ExternalInput").ap()
    dwq = nc.dram_tensor("wq", [F3, E], F32, kind="ExternalInput").ap()
    dbq = nc.dram_tensor("bq", [1, F3], F32, kind="ExternalInput").ap()
    dwo = nc.dram_tensor("wo", [E, E], F32, kind="ExternalInput").ap()
    dbo = nc.dram_tensor("bo", [1, E], F32, kind="ExternalInput").ap()
    dw1 = nc.dram_tensor("w1", [FF, E], F32, kind="ExternalInput").ap()
    db1 = nc.dram_tensor("b1", [1, FF], F32, kind="ExternalInput").ap()
    dw2 = nc.dram_tensor("w2", [E, FF], F32, kind="ExternalInput").ap()
    db2 = nc.dram_tensor("b2", [1, E], F32, kind="ExternalInput").ap()
    dident = nc.dram_tensor("ident", [128, 128], BF16, kind="ExternalInput").ap()
    dmask = nc.dram_tensor("mask4", [128, 512], BF16, kind="ExternalInput").ap()
    dout = nc.dram_tensor("out", [S_LEN, E], F32, kind="ExternalOutput").ap()
    dO = nc.dram_tensor("oscr", [H, S_LEN, D], BF16, kind="Internal").ap()
    dW2T = nc.dram_tensor("w2tscr", [128, 32 * 1024], BF16, kind="Internal").ap()

    with ExitStack() as ctx:
        arena = ctx.enter_context(nc.sbuf_tensor("arena", [128, ARENA_COLS], BF16))
        arena2 = ctx.enter_context(nc.sbuf_tensor("arena2", [128, ARENA2_F32], F32))

        def bfv(off, cols, parts=128, p0=0):
            return arena[p0:p0 + parts, off:off + cols]

        def f32v(off, cols, parts=128, p0=0):
            return arena2[p0:p0 + parts, off // 2:off // 2 + cols]

        sbt = lambda n_, s_, d_=F32: ctx.enter_context(nc.sbuf_tensor(n_, s_, d_))
        ident = sbt("identb", [128, 128], BF16)
        mask4 = bfv(O_MSK, 512)
        onesb = sbt("onesb", [1, 128], BF16)
        onesf = sbt("onesf", [128, 1], F32)
        onesfr = sbt("onesfr", [1, 128], F32)
        b1r = bfv(R_ROWS, FF, parts=1, p0=0)
        bqr = bfv(R_ROWS, F3, parts=1, p0=32)
        onesb32 = bfv(R_ROWS + 3072, 128, parts=1, p0=32)
        bor = bfv(R_ROWS, E, parts=1, p0=64)
        b2r = bfv(R_ROWS + 1024, E, parts=1, p0=64)
        onesb64 = bfv(R_ROWS + 2048, 128, parts=1, p0=64)
        epsc = sbt("epsc", [128, 1], F32)
        halfp = sbt("halfp", [128, 1], F32)
        halfn = sbt("halfn", [128, 1], F32)
        m128c = sbt("m128c", [128, 1], F32)
        c_sum = sbt("c_sum", [128, 1], F32)
        c_ss = sbt("c_ss", [128, 1], F32)
        c_mean = sbt("c_mean", [128, 1], F32)
        c_var = sbt("c_var", [128, 1], F32)
        c_rstd = sbt("c_rstd", [128, 1], F32)
        c_nb = sbt("c_nb", [128, 1], F32)
        c_mx = sbt("c_mx", [128, 1], F32)
        c_mn = sbt("c_mn", [128, 1], F32)
        c_t1 = sbt("c_t1", [128, 1], F32)
        c_t2 = sbt("c_t2", [128, 1], F32)
        c_alpha = sbt("c_alpha", [128, 1], F32)
        c_sact = sbt("c_sact", [128, 1], F32)
        c_sinv = sbt("c_sinv", [128, 1], F32)
        c_ca = sbt("c_ca", [128, 1], F32)
        c_cb = sbt("c_cb", [128, 1], F32)
        c_rec = sbt("c_rec", [128, 4], F32)
        q_acc1 = sbt("q_acc1", [128, 1], F32)
        q_acc2 = sbt("q_acc2", [128, 1], F32)
        q_tmp = sbt("q_tmp", [128, 1], F32)
        q_s1 = sbt("q_s1", [128, 1], F32)
        q_s2 = sbt("q_s2", [128, 1], F32)
        q_m1 = sbt("q_m1", [128, 1], F32)
        q_m2 = sbt("q_m2", [128, 1], F32)
        q_sc = sbt("q_sc", [1, 2], F32)
        c_em = sbt("c_em", [128, 1], F32)
        c_ev = sbt("c_ev", [128, 1], F32)
        c_er = sbt("c_er", [128, 1], F32)
        c_enb = sbt("c_enb", [128, 1], F32)
        c_eca = sbt("c_eca", [128, 1], F32)
        c_es1 = sbt("c_es1", [128, 1], F32)
        c_es2 = sbt("c_es2", [128, 1], F32)
        junk = bfv(B_JNK, 512)
        c_ss2 = sbt("c_ss2", [128, 1], F32)
        c_h = sbt("c_h", [128, 8], F32)

        ps = [ctx.enter_context(nc.psum_tensor("ps%d" % i, [128, 512], F32))
              for i in range(8)]

        sems = {}
        for name in ["pDma0", "pDma1", "pC", "pCast", "pTr", "pTre", "pMean", "pMeanE", "pBias",
                     "pSgn", "pCmb", "pW2d",
                     "aX0", "aX1", "aLN", "aT", "aTT", "aTTe", "aQKV", "aQKVe", "aPK0", "aPK1", "aPK2", "aPK3", "aPV0", "aPV1", "aPV2", "aPV3",
                     "aQKT", "aQKTe", "aGR", "aEXP", "aMSK", "aAV", "aAVe",
                     "aOT", "aDIV", "aOW", "wW2",
                     "bO", "bX", "bTT", "bTTe", "bWo", "bYs", "bYa", "bZ",
                     "bZs", "bCA", "bCF", "bUT", "bUTe", "bW1", "bGel", "bHs",
                     "bVT", "bVTe", "bW2", "bOs", "bOa", "bOut", "bAdd", "bSq", "aSq", "vS", "sS", "sSG", "vFD"]:
            sems[name] = ctx.enter_context(nc.semaphore(name))
        SM = sems

        with nc.Block() as blk:

            @blk.sync
            def _(sync):
                ld = 0
                srcs = []
                for i in range(N_WQ):
                    srcs.append(dwq[128 * i:128 * (i + 1), :])
                for i in range(N_WO):
                    srcs.append(dwo[128 * i:128 * (i + 1), :])
                w1srcs = [dw1[128 * i:128 * (i + 1), :] for i in range(N_W1)]
                w2srcs = [dw2[128 * (i // 4):128 * (i // 4) + 128,
                              1024 * (i % 4):1024 * (i % 4 + 1)] for i in range(N_W2)]
                srcs += w1srcs + w2srcs
                for src in srcs:
                    if ld >= 2:
                        sync.wait_ge(SM["pCast"], ld - 1)
                    sync.dma_start(out=f32v(O_STAG + 2048 * (ld % 2), 1024),
                                   in_=src).then_inc(SM["pDma%d" % (ld % 2)], 16)
                    ld += 1
                for k, src in enumerate(w1srcs + w2srcs):
                    if k < 2:
                        sync.wait_ge(SM["pCast"], NPRE + NSUM)
                    else:
                        sync.wait_ge(SM["pSgn"], 2 * (k - 1))
                    sync.dma_start(out=f32v(O_STAG + 2048 * (ld % 2), 1024),
                                   in_=src).then_inc(SM["pDma%d" % (ld % 2)], 16)
                    ld += 1
                def qk_packs(ii):
                    sync.wait_ge(SM["aQKVe"], 6 * ii + 2)
                    if ii >= 1:
                        sync.wait_ge(SM["aQKT"], 8 * ii)
                    for grp in range(16):
                        sync.dma_start(out=bfv(O_QP + 64 * grp, 64),
                                       in_=bfv(O_QKV, 1024, parts=8, p0=8 * grp)
                                       ).then_inc(SM["aPK%d" % (grp // 4)], 16)
                    sync.wait_ge(SM["aQKVe"], 6 * ii + 4)
                    for grp in range(16):
                        sync.dma_start(out=bfv(O_KP + 64 * grp, 64),
                                       in_=bfv(O_QKV + 1024, 1024, parts=8, p0=8 * grp)
                                       ).then_inc(SM["aPK%d" % (grp // 4)], 16)

                for i in range(NT):
                    if i >= 2:
                        sync.wait_ge(SM["aT"], 2 * (i - 1))
                    sync.dma_start(out=f32v(O_XF + 2048 * (i % 2), 1024),
                                   in_=dx[128 * i:128 * (i + 1), :]).then_inc(SM["aX%d" % (i % 2)], 16)
                    if i >= 1:
                        qk_packs(i - 1)
                qk_packs(NT - 1)
                sync.wait_ge(SM["aOW"], 256 * NT)
                sync.wait_ge(SM["aQKV"], 6 * NT)
                sync.wait_ge(SM["pW2d"], 16 * 8 * N_W2)
                for kt in range(32):
                    sync.dma_start(out=bfv(O_W2T + 1024 * kt, 1024),
                                   in_=dW2T[:, 1024 * kt:1024 * (kt + 1)]).then_inc(SM["wW2"], 16)
                o2flat = dO.rearrange("h s d -> (h s) d").rearrange("(r c) d -> r (c d)", c=16)
                for j in range(NT):
                    sync.wait_ge(SM["aOW"], 256 * NT)
                    if j >= 1:
                        sync.wait_ge(SM["bTT"], j)
                    sync.dma_start(out=bfv(B_OUT2, 1024),
                                   in_=o2flat[128 * j:128 * (j + 1), :]).then_inc(SM["bO"], 16)

            @blk.vector
            def _(v_raw):
                v = SEng(v_raw, SM["vS"])
                v.memset(onesb[:], 1.0)
                v.memset(onesb32, 1.0)
                v.memset(onesb64, 1.0)
                v.memset(onesf[:], 1.0)
                v.memset(onesfr[:], 1.0)
                v.memset(epsc[:], EPS)
                v.memset(halfp[:], 0.5)
                v.memset(halfn[:], -0.5)
                v.memset(m128c[:], -128.0)
                v.memset(q_acc1[:], 0.0)
                v.memset(q_acc2[:], 0.0)
                for g in range(16):
                    v.memset(bfv(O_VP + 65 * g + 64, 1), 1.0)
                for i in range(NPRE):
                    v.wait_ge(SM["pDma%d" % (i % 2)], 16 * (i // 2 + 1))
                    if i >= 1:
                        v.wait_ge(SM["pTr"], i)
                    v.tensor_copy(bfv(O_SP, 1024),
                                  f32v(O_STAG + 2048 * (i % 2), 1024)).then_inc(SM["pCast"], 1)
                for k in range(NSUM):
                    ld = NPRE + k
                    v.wait_ge(SM["pDma%d" % (ld % 2)], 16 * (ld // 2 + 1))
                    acc = q_acc1 if k < N_W1 else q_acc2
                    v.tensor_reduce(q_tmp[:], f32v(O_STAG + 2048 * (ld % 2), 1024),
                                    AX, ALU.add, apply_absolute_value=True)
                    v.tensor_add(acc[:], acc[:], q_tmp[:]).then_inc(SM["pCast"], 1)
                v.wait_ge(SM["pMeanE"], 2)
                for (col_s, col_m, nelem) in ((q_s1, q_m1, FF * E), (q_s2, q_m2, E * FF)):
                    v.tensor_scalar(col_m[:], col_s[:], 1.0 / nelem, None, ALU.mult)
                    v.tensor_scalar_max(col_m[:], col_m[:], 1e-5)
                    v.reciprocal(col_s[:], col_m[:])
                v.wait_ge(SM["pC"], 96)
                v.tensor_scalar(b1r, b1r, q_s1[0:1, 0:1], None, ALU.mult)
                v.tensor_scalar(b2r, b2r, q_s2[0:1, 0:1], None,
                                ALU.mult).then_inc(SM["pMean"], 1)
                for k in range(NQNT):
                    v.wait_ge(SM["pSgn"], 2 * (k + 1))
                    v.tensor_add(bfv(O_SP, 1024), bfv(O_SP, 1024), bfv(O_SM, 1024))
                    v.tensor_scalar(bfv(O_SP, 1024), bfv(O_SP, 1024), 0.5, None,
                                    ALU.mult).then_inc(SM["pCmb"], 1)

                for i in range(NT):
                    xf = f32v(O_XF + 2048 * (i % 2), 1024)
                    v.wait_ge(SM["aX%d" % (i % 2)], 16 * (i // 2 + 1))
                    v.wait_ge(SM["aSq"], 2 * (i + 1))
                    v.tensor_add(c_ss[:], c_h[:, 0:1], c_h[:, 1:2])
                    v.tensor_reduce(c_sum[:], xf, AX, ALU.add)
                    v.tensor_scalar(c_mean[:], c_sum[:], 1.0 / E, None, ALU.mult)
                    v.tensor_scalar(c_var[:], c_ss[:], 1.0 / E, None, ALU.mult)
                    v.tensor_mul(c_t1[:], c_mean[:], c_mean[:])
                    v.tensor_sub(c_var[:], c_var[:], c_t1[:]).then_inc(SM["aLN"], 1)
                    v.wait_ge(SM["aT"], 2 * i + 1)
                    v.reciprocal(c_rstd[:], c_rstd[:])
                    v.tensor_mul(c_nb[:], c_mean[:], c_rstd[:])
                    v.tensor_scalar(c_nb[:], c_nb[:], -1.0, None,
                                    ALU.mult).then_inc(SM["aLN"], 1)
                    for b in range(4):
                        v.wait_ge(SM["aEXP"], 4 * i + b + 1)
                        v.tensor_mul(bfv(O_S + 512 * b, 512), bfv(O_S + 512 * b, 512),
                                     mask4[:, :]).then_inc(SM["aMSK"], 1)
                    for b in range(4):
                        v.wait_ge(SM["aOT"], 4 * i + b + 1)
                        if i >= 1 and b == 0:
                            v.wait_ge(SM["aOW"], 256 * i)
                        pt3 = ps[b % 2][:, 0:260].rearrange("p (g c) -> p g c", g=4)
                        v.reciprocal(c_rec[:, 0:4].rearrange("p (g c) -> p g c", c=1),
                                     pt3[:, :, 64:65])
                        v.tensor_tensor(
                            bfv(O_OSB + 256 * b, 256).rearrange("p (g d) -> p g d", g=4),
                            pt3[:, :, 0:64],
                            c_rec[:, :, None].broadcast_to([128, 4, 64]),
                            ALU.mult).then_inc(SM["aDIV"], 1)

                casc_sg = [0]

                def epiV(jj):
                    v.wait_ge(SM["bW2"], 8 * (jj + 1))
                    v.wait_ge(SM["bSq"], 4 * jj + 4)
                    v.tensor_add(c_ev[:], c_es1[:], c_es2[:])
                    v.tensor_reduce(c_em[:], ps[6][:, :], AX, ALU.add)
                    v.tensor_reduce(c_t2[:], ps[7][:, :], AX, ALU.add)
                    v.tensor_add(c_em[:], c_em[:], c_t2[:])
                    v.tensor_scalar(c_em[:], c_em[:], 1.0 / E, None, ALU.mult)
                    v.tensor_scalar(c_ev[:], c_ev[:], 1.0 / E, None, ALU.mult)
                    v.tensor_mul(c_t2[:], c_em[:], c_em[:])
                    v.tensor_sub(c_ev[:], c_ev[:], c_t2[:])
                    v.tensor_mul(c_t2[:], q_m2[:], q_m2[:])
                    v.tensor_mul(c_ev[:], c_ev[:], c_t2[:]).then_inc(SM["bOs"], 1)
                    v.wait_ge(SM["bYa"], 7 * jj + 7)
                    v.reciprocal(c_er[:], c_er[:])
                    v.tensor_mul(c_eca[:], q_m2[:], c_er[:])
                    v.tensor_mul(c_enb[:], c_em[:], c_eca[:])
                    v.tensor_scalar(c_enb[:], c_enb[:], -1.0, None,
                                    ALU.mult).then_inc(SM["bOs"], 1)
                    v.wait_ge(SM["bOa"], jj + 1)
                    if jj >= 1:
                        v.wait_ge(SM["bAdd"], 16 * jj)
                    v.tensor_add(f32v(B_OST, 1024), f32v(B_Z, 1024),
                                 bfv(B_SGN, 1024)).then_inc(SM["bOut"], 1)

                for j in range(NT):
                    zv = f32v(B_Z, 1024)
                    if j >= 1:
                        epiV(j - 1)
                    v.wait_ge(SM["bWo"], 2 * j + 2)
                    v.wait_ge(SM["bSq"], 4 * j + 2)
                    v.tensor_add(c_ss[:], c_h[:, 0:1], c_h[:, 1:2])
                    v.tensor_reduce(c_sum[:], ps[0][:, :], AX, ALU.add)
                    v.tensor_reduce(c_t1[:], ps[1][:, :], AX, ALU.add)
                    v.tensor_add(c_sum[:], c_sum[:], c_t1[:])
                    v.tensor_scalar(c_mean[:], c_sum[:], 1.0 / E, None, ALU.mult)
                    v.tensor_scalar(c_var[:], c_ss[:], 1.0 / E, None, ALU.mult)
                    v.tensor_mul(c_t1[:], c_mean[:], c_mean[:])
                    v.tensor_sub(c_var[:], c_var[:], c_t1[:]).then_inc(SM["bYs"], 1)
                    v.wait_ge(SM["bYa"], 7 * j + 1)
                    v.reciprocal(c_rstd[:], c_rstd[:])
                    v.tensor_mul(c_nb[:], c_mean[:], c_rstd[:])
                    v.tensor_scalar(c_nb[:], c_nb[:], -1.0, None,
                                    ALU.mult).then_inc(SM["bYs"], 1)
                    v.wait_ge(SM["bYa"], 7 * j + 3)
                    v.wait_ge(SM["bX"], 16 * (j + 1))
                    v.tensor_add(zv, bfv(B_XB, 1024), bfv(B_SCR, 1024)).then_inc(SM["bZ"], 1)
                    v.scalar_tensor_tensor(junk[:, :], f32v(B_Z, 512), 1.0,
                                           f32v(B_Z, 512), ALU.mult, ALU.mult,
                                           accum_out=c_h[:, 2:3])
                    v.scalar_tensor_tensor(junk[:, :], f32v(B_Z + 1024, 512), 1.0,
                                           f32v(B_Z + 1024, 512), ALU.mult, ALU.mult,
                                           accum_out=c_h[:, 3:4])
                    v.tensor_add(c_ss[:], c_h[:, 2:3], c_h[:, 3:4])
                    v.tensor_reduce(c_sum[:], zv, AX, ALU.add)
                    v.tensor_reduce(c_mx[:], zv, AX, ALU.max)
                    v.tensor_reduce(c_mn[:], zv, AX, ALU.min)
                    v.tensor_scalar(c_mean[:], c_sum[:], 1.0 / E, None, ALU.mult)
                    v.tensor_scalar(c_var[:], c_ss[:], 1.0 / E, None, ALU.mult)
                    v.tensor_mul(c_t1[:], c_mean[:], c_mean[:])
                    v.tensor_sub(c_var[:], c_var[:], c_t1[:]).then_inc(SM["bZs"], 1)
                    v.wait_ge(SM["bYa"], 7 * j + 4)
                    v.reciprocal(c_rstd[:], c_rstd[:])
                    v.tensor_mul(c_t1[:], c_rstd[:], c_rstd[:])
                    v.tensor_mul(c_t2[:], c_var[:], c_t1[:]).then_inc(SM["bZs"], 1)
                    v.wait_ge(SM["bYa"], 7 * j + 5)
                    v.reciprocal(c_alpha[:], c_alpha[:])
                    v.tensor_sub(c_t1[:], c_mx[:], c_mean[:])
                    v.tensor_sub(c_t2[:], c_mean[:], c_mn[:])
                    v.tensor_max(c_t1[:], c_t1[:], c_t2[:])
                    v.tensor_mul(c_t1[:], c_t1[:], c_rstd[:])
                    v.tensor_mul(c_t1[:], c_t1[:], c_alpha[:])
                    v.tensor_scalar_max(c_t1[:], c_t1[:], 1e-5)
                    v.reciprocal(c_sact[:], c_t1[:])
                    v.tensor_scalar(c_sact[:], c_sact[:], 127.0, None, ALU.mult)
                    v.reciprocal(c_sinv[:], c_sact[:])
                    v.tensor_mul(c_ca[:], c_rstd[:], c_alpha[:])
                    v.tensor_mul(c_ca[:], c_ca[:], c_sact[:])
                    v.tensor_mul(c_cb[:], c_mean[:], c_ca[:])
                    v.tensor_scalar(c_cb[:], c_cb[:], -1.0, 128.5, ALU.mult,
                                    ALU.add).then_inc(SM["bZs"], 1)


                    def cascade_pair(chs, cids, int_sems, int_vals):
                        y0s = [f32v(B_Y0 + 1024 * (c % 2), 512) for c in chs]
                        rrs = [f32v(B_RES + 1024 * (c % 2), 512) for c in chs]
                        sgs = [bfv(B_SGN + 512 * (c % 2), 512) for c in chs]
                        for kb in range(7, -1, -1):
                            for ii in range(len(chs)):
                                casc_sg[0] += 1
                                v.wait_ge(SM["sSG"], casc_sg[0])
                                src = y0s[ii] if kb == 7 else rrs[ii]
                                v.scalar_tensor_tensor(
                                    rrs[ii], sgs[ii], -float(2 ** (kb - 1)), src,
                                    ALU.mult, ALU.add, _nowait=True).then_inc(SM["vFD"], 1)
                        for ii in range(len(chs)):
                            scr = f32v(F_SCR, 512)
                            v.scalar_tensor_tensor(scr, rrs[ii], -1.0, y0s[ii],
                                                   ALU.mult, ALU.add)
                            v.tensor_scalar(scr, scr, -0.5, None, ALU.add)
                            if int_vals[ii] > 0:
                                v.wait_ge(SM[int_sems[ii]], int_vals[ii])
                            v.tensor_scalar(bfv(B_INT, 512), scr, c_sinv[:, 0:1],
                                            None, ALU.mult).then_inc(SM["bCF"], 1)

                    v.wait_ge(SM["bVT"], 8 * j)
                    cascade_pair([0, 1], [10 * j, 10 * j + 1],
                                 ["bUT", "bUT"], [2 * j, 2 * j + 1])
                    v.wait_ge(SM["bGel"], 8 * (j + 1))
                    for c_ in range(8):
                        hgc = bfv(B_HG + 512 * c_, 512)
                        v.scalar_tensor_tensor(junk[:, :], hgc, 1.0, hgc,
                                               ALU.mult, ALU.mult,
                                               accum_out=c_h[:, c_:c_ + 1])
                    v.tensor_reduce(c_ss[:], c_h[:, 0:8], AX, ALU.add)
                    v.tensor_reduce(c_mx[:], bfv(B_HG, 4096), AX, ALU.max,
                                    apply_absolute_value=True)
                    v.tensor_scalar(c_var[:], c_ss[:], 1.0 / FF, None,
                                    ALU.mult).then_inc(SM["bHs"], 1)
                    v.wait_ge(SM["bYa"], 7 * j + 6)
                    v.reciprocal(c_alpha[:], c_alpha[:])
                    v.tensor_mul(c_t1[:], c_mx[:], c_alpha[:])
                    v.tensor_scalar_max(c_t1[:], c_t1[:], 1e-5)
                    v.reciprocal(c_sact[:], c_t1[:])
                    v.tensor_scalar(c_sact[:], c_sact[:], 127.0, None, ALU.mult)
                    v.reciprocal(c_sinv[:], c_sact[:])
                    v.tensor_mul(c_ca[:], c_alpha[:], c_sact[:])
                    v.memset(c_cb[:], 128.5).then_inc(SM["bHs"], 1)
                    for pp_ in range(4):
                        ca, cb = 2 * pp_, 2 * pp_ + 1
                        cascade_pair([ca, cb], [10 * j + 2 + ca, 10 * j + 2 + cb],
                                     ["bVT", "bVT"], [8 * j + ca, 8 * j + cb])


                epiV(NT - 1)

            @blk.scalar
            def _(s_raw):
                s = SEng(s_raw, SM["sS"])
                for i in range(NPRE):
                    s.wait_ge(SM["pTr"], i + 1)
                    for half in range(2):
                        for q in range(4):
                            et = half * 4 + q
                            if i < N_WQ:
                                dst = bfv(O_WQT + et * F3 + i * 128, 128)
                            else:
                                dst = bfv(O_WOT + et * E + (i - N_WQ) * 128, 128)
                            ins = s.activation(dst, ps[half][:, 128 * q:128 * (q + 1)],
                                               ACT.Copy)
                    ins.then_inc(SM["pTre"], 1)
                s.wait_ge(SM["pTr"], NPRE + 1)
                s.activation(q_sc[0:1, 0:2], ps[4][0:1, 0:2], ACT.Copy).then_inc(SM["pMeanE"], 1)
                s.wait_ge(SM["pTr"], NPRE + 2)
                s.activation(q_s1[:, 0:1], ps[5][:, 0:1], ACT.Copy)
                s.activation(q_s2[:, 0:1], ps[5][:, 1:2], ACT.Copy).then_inc(SM["pMeanE"], 1)
                for k in range(NQNT):
                    ld = NPRE + NSUM + k
                    s.wait_ge(SM["pDma%d" % (ld % 2)], 16 * (ld // 2 + 1))
                    s.wait_ge(SM["pMean"], 1)
                    if k == 0:
                        s.wait_ge(SM["pTr"], NPRE)
                    else:
                        s.wait_ge(SM["pCmb"], k)
                        s.wait_ge(SM["pTr"], NPRE + 2 + k)
                    scol = q_s1 if k < N_W1 else q_s2
                    src = f32v(O_STAG + 2048 * (ld % 2), 1024)
                    s.activation(bfv(O_SP, 1024), src, ACT.Sign, bias=halfn[:, 0:1],
                                 scale=scol[:, 0:1])
                    s.activation(bfv(O_SM, 1024), src, ACT.Sign, bias=halfp[:, 0:1],
                                 scale=scol[:, 0:1]).then_inc(SM["pSgn"], 2)
                    s.wait_ge(SM["pTr"], NPRE + 2 + k + 1)
                    if k >= N_W1:
                        s.wait_ge(SM["pW2d"], 16 * 8 * (k - N_W1))
                    for half in range(2):
                        for q in range(4):
                            et = half * 4 + q
                            if k < N_W1:
                                dst = bfv(O_W1T + et * FF + k * 128, 128)
                            else:
                                dst = bfv(O_W2S + et * 128, 128)
                            ins = s.activation(dst, ps[half][:, 128 * q:128 * (q + 1)],
                                               ACT.Copy)
                    ins.then_inc(SM["pTre"], 1)

                for i in range(NT):
                    xf = f32v(O_XF + 2048 * (i % 2), 1024)
                    s.wait_ge(SM["aX%d" % (i % 2)], 16 * (i // 2 + 1))
                    if i >= 1:
                        s.wait_ge(SM["aLN"], 2 * i - 1)
                    s.activation(bfv(O_JNKA, 512), xf[:, 0:512], ACT.Square,
                                 accum_out=c_h[:, 0:1]).then_inc(SM["aSq"], 1)
                    s.activation(bfv(O_JNKA, 512), xf[:, 512:1024], ACT.Square,
                                 accum_out=c_h[:, 1:2]).then_inc(SM["aSq"], 1)
                    s.wait_ge(SM["aLN"], 2 * i + 1)
                    s.activation(c_rstd[:], c_var[:], ACT.Sqrt,
                                 bias=epsc[:, 0:1]).then_inc(SM["aT"], 1)
                    s.wait_ge(SM["aLN"], 2 * i + 2)
                    if i >= 1:
                        s.wait_ge(SM["aTT"], i)
                    s.activation(bfv(O_TNAT, 1024), xf, ACT.Identity,
                                 bias=c_nb[:, 0:1], scale=c_rstd[:, 0:1]).then_inc(SM["aT"], 1)
                    s.wait_ge(SM["aTT"], i + 1)
                    s.activation(bfv(O_TT, 512), ps[2][:, :], ACT.Copy)
                    s.activation(bfv(O_TT + 512, 512), ps[3][:, :],
                                 ACT.Copy).then_inc(SM["aTTe"], 1)
                    for ch in range(6):
                        s.wait_ge(SM["aQKV"], 6 * i + ch + 1)
                        s.activation(bfv(O_QKV + 512 * ch, 512), ps[2 + ch % 2][:, :],
                                     ACT.Copy).then_inc(SM["aQKVe"], 1)
                    for b in range(8):
                        s.wait_ge(SM["aQKT"], 8 * i + b + 1)
                        if b == 0 and i >= 1:
                            s.wait_ge(SM["aGR"], 4 * i)
                        dsto = O_QT if b < 4 else O_KT
                        s.activation(bfv(dsto + 512 * (b % 4), 512, parts=64),
                                     ps[b % 2][0:64, :], ACT.Copy).then_inc(SM["aQKTe"], 1)
                    for b in range(4):
                        s.wait_ge(SM["aGR"], 4 * i + b + 1)
                        if b == 0 and i >= 1:
                            s.wait_ge(SM["aAV"], 4 * i)
                        s.activation(bfv(O_S + 512 * b, 512), ps[4 + b % 2][:, :],
                                     ACT.Exp, scale=0.03125).then_inc(SM["aEXP"], 1)
                    for b in range(4):
                        s.wait_ge(SM["aAV"], 4 * i + b + 1)
                        if b == 0 and i >= 1:
                            s.wait_ge(SM["aOT"], 4 * i)
                        s.activation(bfv(O_AV + 512 * b, 512, parts=65),
                                     ps[6 + b % 2][0:65, :], ACT.Copy).then_inc(SM["aAVe"], 1)

                sg_sched = [0]

                def epiS(jj):
                    s.wait_ge(SM["bW2"], 8 * (jj + 1))
                    s.activation(junk[:, :], ps[6][:, :], ACT.Square,
                                 accum_out=c_es1[:, 0:1]).then_inc(SM["bSq"], 1)
                    s.activation(junk[:, :], ps[7][:, :], ACT.Square,
                                 accum_out=c_es2[:, 0:1]).then_inc(SM["bSq"], 1)
                    s.wait_ge(SM["bOs"], 2 * jj + 1)
                    s.activation(c_er[:], c_ev[:], ACT.Sqrt,
                                 bias=epsc[:, 0:1]).then_inc(SM["bYa"], 1)
                    s.wait_ge(SM["bOs"], 2 * jj + 2)
                    s.activation(bfv(B_SGN, 512), ps[6][:, :], ACT.Identity,
                                 bias=c_enb[:, 0:1], scale=c_eca[:, 0:1])
                    s.activation(bfv(B_SGN + 512, 512), ps[7][:, :], ACT.Identity,
                                 bias=c_enb[:, 0:1], scale=c_eca[:, 0:1]).then_inc(SM["bOa"], 1)

                for j in range(NT):
                    if j >= 1:
                        epiS(j - 1)
                    s.wait_ge(SM["bTT"], j + 1)
                    if j >= 1:
                        s.wait_ge(SM["bWo"], 2 * j)
                    s.activation(bfv(B_O2T, 512), ps[0][:, :], ACT.Copy)
                    s.activation(bfv(B_O2T + 512, 512), ps[1][:, :],
                                 ACT.Copy).then_inc(SM["bTTe"], 1)
                    s.wait_ge(SM["bWo"], 2 * j + 2)
                    if j >= 1:
                        s.wait_ge(SM["bHs"], 2 * j - 1)
                    s.activation(junk[:, :], ps[0][:, :], ACT.Square,
                                 accum_out=c_h[:, 0:1]).then_inc(SM["bSq"], 1)
                    s.activation(junk[:, :], ps[1][:, :], ACT.Square,
                                 accum_out=c_h[:, 1:2]).then_inc(SM["bSq"], 1)
                    s.wait_ge(SM["bYs"], 2 * j + 1)
                    s.activation(c_rstd[:], c_var[:], ACT.Sqrt,
                                 bias=epsc[:, 0:1]).then_inc(SM["bYa"], 1)
                    s.wait_ge(SM["bYs"], 2 * j + 2)
                    s.activation(bfv(B_SCR, 512), ps[0][:, :], ACT.Identity,
                                 bias=c_nb[:, 0:1], scale=c_rstd[:, 0:1]).then_inc(SM["bYa"], 1)
                    s.activation(bfv(B_SCR + 512, 512), ps[1][:, :], ACT.Identity,
                                 bias=c_nb[:, 0:1], scale=c_rstd[:, 0:1]).then_inc(SM["bYa"], 1)
                    s.wait_ge(SM["bZs"], 3 * j + 1)
                    s.activation(c_rstd[:], c_var[:], ACT.Sqrt,
                                 bias=epsc[:, 0:1]).then_inc(SM["bYa"], 1)
                    s.wait_ge(SM["bZs"], 3 * j + 2)
                    s.activation(c_alpha[:], c_t2[:], ACT.Sqrt,
                                 bias=epsc[:, 0:1]).then_inc(SM["bYa"], 1)
                    zv = f32v(B_Z, 1024)

                    def affine_pair(srcs, cids):
                        for src, cid in zip(srcs, cids):
                            if cid >= 2:
                                s.wait_ge(SM["bCF"], cid - 1)
                            s.activation(f32v(B_Y0 + 1024 * (cid % 2), 512),
                                         src, ACT.Identity,
                                         bias=c_cb[:, 0:1], scale=c_ca[:, 0:1]).then_inc(SM["bCA"], 1)

                    def sign_pair(chs):
                        sgs = [bfv(B_SGN + 512 * (c % 2), 512) for c in chs]
                        rrs = [f32v(B_RES + 1024 * (c % 2), 512) for c in chs]
                        y0s = [f32v(B_Y0 + 1024 * (c % 2), 512) for c in chs]
                        for kb in range(7, -1, -1):
                            for ii in range(len(chs)):
                                if kb == 7:
                                    s.wait_ge(SM["vFD"], sg_sched[0])
                                    src = y0s[ii]
                                else:
                                    sg_sched[0] += 1
                                    s.wait_ge(SM["vFD"], sg_sched[0])
                                    src = rrs[ii]
                                s.activation(sgs[ii], src, ACT.Sign,
                                             bias=m128c[:, 0:1], _nowait=(kb < 7)).then_inc(SM["sSG"], 1)
                        sg_sched[0] += len(chs)

                    s.wait_ge(SM["bZs"], 3 * j + 3)
                    if j >= 1:
                        s.wait_ge(SM["bOut"], j)
                    affine_pair([zv[:, 0:512], zv[:, 512:1024]], [10 * j, 10 * j + 1])
                    sign_pair([0, 1])
                    for cch in range(2):
                        s.wait_ge(SM["bUT"], 2 * j + cch + 1)
                        if j >= 1 and cch == 0:
                            s.wait_ge(SM["bW1"], 8 * j)
                        s.activation(bfv(B_UQT + 512 * cch, 512), ps[2 + cch][:, :],
                                     ACT.Copy).then_inc(SM["bUTe"], 1)
                    for ch in range(8):
                        s.wait_ge(SM["bW1"], 8 * j + ch + 1)
                        if j >= 1 and ch == 0:
                            s.wait_ge(SM["bCF"], 10 * j)
                        s.activation(bfv(B_HG + 512 * ch, 512), ps[4 + ch % 2][:, :],
                                     ACT.Gelu, scale=q_m1[:, 0:1]).then_inc(SM["bGel"], 1)
                    s.wait_ge(SM["bHs"], 2 * j + 1)
                    s.activation(c_alpha[:], c_var[:], ACT.Sqrt,
                                 bias=epsc[:, 0:1]).then_inc(SM["bYa"], 1)
                    def vq_evac(cch):
                        s.wait_ge(SM["bVT"], 8 * j + cch + 1)
                        if j == 0 and cch == 0:
                            s.wait_ge(SM["pW2d"], 16 * 8 * N_W2)
                        if cch >= 2:
                            s.wait_ge(SM["bW2"], 8 * j + cch - 1)
                        elif j >= 1:
                            s.wait_ge(SM["bW2"], 8 * j)
                        s.activation(bfv(B_VQT + 512 * (cch % 2), 512), ps[2 + cch % 2][:, :],
                                     ACT.Copy).then_inc(SM["bVTe"], 1)

                    s.wait_ge(SM["bHs"], 2 * j + 2)
                    for pp_ in range(4):
                        ca, cb = 2 * pp_, 2 * pp_ + 1
                        affine_pair([bfv(B_HG + 512 * ca, 512), bfv(B_HG + 512 * cb, 512)],
                                    [10 * j + 2 + ca, 10 * j + 2 + cb])
                        sign_pair([ca, cb])
                        if pp_ >= 1:
                            vq_evac(2 * pp_ - 2)
                            vq_evac(2 * pp_ - 1)
                    vq_evac(6)
                    vq_evac(7)


                epiS(NT - 1)

            @blk.tensor
            def _(t):
                for i in range(NPRE):
                    t.wait_ge(SM["pCast"], i + 1)
                    if i == 0:
                        t.wait_ge(SM["pC"], 96)
                    if i >= 1:
                        t.wait_ge(SM["pTre"], i)
                    for half in range(2):
                        for q in range(4):
                            et = half * 4 + q
                            ins = t.matmul(ps[half][:, 128 * q:128 * (q + 1)],
                                           bfv(O_SP + 128 * et, 128), ident[:, :],
                                           start=True, stop=True)
                    ins.then_inc(SM["pTr"], 1)
                t.wait_ge(SM["pCast"], NPRE + NSUM)
                t.matmul(ps[4][0:1, 0:1], q_acc1[:, 0:1], onesf[:, 0:1], start=True, stop=True)
                t.matmul(ps[4][0:1, 1:2], q_acc2[:, 0:1], onesf[:, 0:1],
                         start=True, stop=True).then_inc(SM["pTr"], 1)
                t.wait_ge(SM["pMeanE"], 1)
                t.matmul(ps[5][:, 0:1], onesfr[0:1, :], q_sc[0:1, 0:1], start=True, stop=True)
                t.matmul(ps[5][:, 1:2], onesfr[0:1, :], q_sc[0:1, 1:2],
                         start=True, stop=True).then_inc(SM["pTr"], 1)
                for k in range(NQNT):
                    t.wait_ge(SM["pCmb"], k + 1)
                    t.wait_ge(SM["pTre"], NPRE + k)
                    for half in range(2):
                        for q in range(4):
                            et = half * 4 + q
                            ins = t.matmul(ps[half][:, 128 * q:128 * (q + 1)],
                                           bfv(O_SP + 128 * et, 128), ident[:, :],
                                           start=True, stop=True)
                    ins.then_inc(SM["pTr"], 1)

                for i in range(NT):
                    t.wait_ge(SM["aT"], 2 * i + 2)
                    if i == 0:
                        t.wait_ge(SM["pTre"], NPRE + NQNT)
                    else:
                        t.wait_ge(SM["aQKVe"], 6 * i)
                    for q in range(8):
                        ins = t.matmul(ps[2 + q // 4][:, 128 * (q % 4):128 * (q % 4 + 1)],
                                       bfv(O_TNAT + 128 * q, 128), ident[:, :],
                                       start=True, stop=True)
                    ins.then_inc(SM["aTT"], 1)
                    for ch in range(6):
                        gc = 6 * i + ch
                        t.wait_ge(SM["aTTe"], i + 1)
                        if gc >= 2:
                            t.wait_ge(SM["aQKVe"], gc - 1)
                        if i >= 1 and ch == 0:
                            for qq in range(4):
                                t.wait_ge(SM["aPK%d" % qq], 128 * i)
                                t.wait_ge(SM["aPV%d" % qq], 64 * i)
                        bank = 2 + ch % 2
                        t.matmul(ps[bank][:, :], onesb32[0:1, :],
                                 bqr[0:1, 512 * ch:512 * (ch + 1)], start=True, stop=False)
                        for et in range(8):
                            ins = t.matmul(ps[bank][:, :], bfv(O_TT + 128 * et, 128),
                                           bfv(O_WQT + et * F3 + 512 * ch, 512),
                                           start=False, stop=(et == 7))
                        ins.then_inc(SM["aQKV"], 1)
                    for b in range(8):
                        po = O_QP if b < 4 else O_KP
                        t.wait_ge(SM["aPK%d" % (b % 4)], 128 * (i + 1))
                        if b < 2 and i >= 1:
                            t.wait_ge(SM["aDIV"], 4 * i)
                        if 8 * i + b >= 2:
                            t.wait_ge(SM["aQKTe"], 8 * i + b - 1)
                        for q in range(4):
                            g = 4 * (b % 4) + q
                            ins = t.matmul(ps[b % 2][0:64, 128 * q:128 * (q + 1)],
                                           bfv(po + 64 * g, 64), ident[:, :],
                                           start=True, stop=True)
                        ins.then_inc(SM["aQKT"], 1)
                    for b in range(4):
                        t.wait_ge(SM["aQKTe"], 8 * i + 5 + b)
                        if 4 * i + b >= 2:
                            t.wait_ge(SM["aEXP"], 4 * i + b - 1)
                        for q in range(4):
                            g = 4 * b + q
                            ins = t.matmul(ps[4 + b % 2][:, 128 * q:128 * (q + 1)],
                                           bfv(O_KT + 128 * g, 128, parts=64),
                                           bfv(O_QT + 128 * g, 128, parts=64),
                                           start=True, stop=True)
                        ins.then_inc(SM["aGR"], 1)
                    for b in range(4):
                        t.wait_ge(SM["aMSK"], 4 * i + b + 1)
                        t.wait_ge(SM["aPV%d" % b], 64 * (i + 1))
                        if 4 * i + b >= 2:
                            t.wait_ge(SM["aAVe"], 4 * i + b - 1)
                        for q in range(4):
                            g = 4 * b + q
                            ins = t.matmul(ps[6 + b % 2][0:65, 128 * q:128 * (q + 1)],
                                           bfv(O_VP + 65 * g, 65),
                                           bfv(O_S + 128 * g, 128),
                                           start=True, stop=True)
                        ins.then_inc(SM["aAV"], 1)
                    for b in range(4):
                        t.wait_ge(SM["aAVe"], 4 * i + b + 1)
                        if b < 2:
                            t.wait_ge(SM["aQKTe"], 8 * i + 8)
                        if 4 * i + b >= 2:
                            t.wait_ge(SM["aDIV"], 4 * i + b - 1)
                        for q in range(4):
                            g = 4 * b + q
                            ins = t.matmul(ps[b % 2][:, 65 * q:65 * (q + 1)],
                                           bfv(O_AV + 128 * g, 128, parts=65),
                                           ident[0:65, 0:65],
                                           start=True, stop=True)
                        ins.then_inc(SM["aOT"], 1)

                for j in range(NT):
                    t.wait_ge(SM["bO"], 16 * (j + 1))
                    if j >= 1:
                        t.wait_ge(SM["bTTe"], j)
                        t.wait_ge(SM["bYa"], 7 * (j - 1) + 3)
                        t.wait_ge(SM["bYs"], 2 * j)
                    for q in range(8):
                        ins = t.matmul(ps[q // 4][:, 128 * (q % 4):128 * (q % 4 + 1)],
                                       bfv(B_OUT2 + 128 * q, 128), ident[:, :],
                                       start=True, stop=True)
                    ins.then_inc(SM["bTT"], 1)
                    t.wait_ge(SM["bTTe"], j + 1)
                    if j >= 1:
                        t.wait_ge(SM["bZ"], j)
                    for ch in range(2):
                        t.matmul(ps[ch][:, :], onesb64[0:1, :],
                                 bor[0:1, 512 * ch:512 * (ch + 1)], start=True, stop=False)
                        for et in range(8):
                            ins = t.matmul(ps[ch][:, :], bfv(B_O2T + 128 * et, 128),
                                           bfv(O_WOT + et * E + 512 * ch, 512),
                                           start=False, stop=(et == 7))
                        ins.then_inc(SM["bWo"], 1)
                    for cch in range(2):
                        t.wait_ge(SM["bCF"], 10 * j + cch + 1)
                        if j >= 1 and cch == 0:
                            t.wait_ge(SM["bVTe"], 8 * j)
                        for q in range(4):
                            ins = t.matmul(ps[2 + cch][:, 128 * q:128 * (q + 1)],
                                           bfv(B_INT + 128 * q, 128), ident[:, :],
                                           start=True, stop=True)
                        ins.then_inc(SM["bUT"], 1)
                    t.wait_ge(SM["bUTe"], 2 * (j + 1))
                    for ch in range(8):
                        if 8 * j + ch >= 2:
                            t.wait_ge(SM["bGel"], 8 * j + ch - 1)
                        bank = 4 + ch % 2
                        t.matmul(ps[bank][:, :], onesb[0:1, :],
                                 b1r[0:1, 512 * ch:512 * (ch + 1)], start=True, stop=False)
                        for et in range(8):
                            ins = t.matmul(ps[bank][:, :], bfv(B_UQT + 128 * et, 128),
                                           bfv(O_W1T + et * FF + 512 * ch, 512),
                                           start=False, stop=(et == 7))
                        ins.then_inc(SM["bW1"], 1)
                    t.wait_ge(SM["bOa"], j)
                    t.wait_ge(SM["bOut"], j)
                    for ch in range(2):
                        t.matmul(ps[6 + ch][:, :], onesb64[0:1, :],
                                 b2r[0:1, 512 * ch:512 * (ch + 1)], start=True, stop=False)
                    for cch in range(8):
                        t.wait_ge(SM["bCF"], 10 * j + 2 + cch + 1)
                        if 8 * j + cch >= 2:
                            t.wait_ge(SM["bVTe"], 8 * j + cch - 1)
                        for q in range(4):
                            ins = t.matmul(ps[2 + cch % 2][:, 128 * q:128 * (q + 1)],
                                           bfv(B_INT + 128 * q, 128), ident[:, :],
                                           start=True, stop=True)
                        ins.then_inc(SM["bVT"], 1)
                        t.wait_ge(SM["bVTe"], 8 * j + cch + 1)
                        if j == 0 and cch == 0:
                            t.wait_ge(SM["wW2"], 512)
                        for qi in range(4):
                            kt = 4 * cch + qi
                            for ch in range(2):
                                ins = t.matmul(ps[6 + ch][:, :],
                                               bfv(B_VQT + 512 * (cch % 2) + 128 * qi, 128),
                                               bfv(O_W2T + 1024 * kt + 512 * ch, 512),
                                               start=False, stop=(cch == 7 and qi == 3))
                        ins.then_inc(SM["bW2"], 1)

            @blk.gpsimd
            def _(g):
                g.dma_start(out=ident[:, :], in_=dident).then_inc(SM["pC"], 16)
                g.dma_start(out=mask4[:, :], in_=dmask).then_inc(SM["pC"], 16)
                g.dma_start(out=bqr[:, :], in_=dbq).then_inc(SM["pC"], 16)
                g.dma_start(out=bor[:, :], in_=dbo).then_inc(SM["pC"], 16)
                g.dma_start(out=b1r, in_=db1).then_inc(SM["pC"], 16)
                g.dma_start(out=b2r, in_=db2).then_inc(SM["pC"], 16)
                for k in range(N_W2):
                    g.wait_ge(SM["pTre"], NPRE + N_W1 + k + 1)
                    eb, cc = k // 4, k % 4
                    for et in range(8):
                        kt = cc * 8 + et
                        g.dma_start(out=dW2T[:, 1024 * kt + 128 * eb:1024 * kt + 128 * (eb + 1)],
                                    in_=bfv(O_W2S + 128 * et, 128)).then_inc(SM["pW2d"], 16)
                for i in range(NT):
                    g.wait_ge(SM["aQKVe"], 6 * i + 6)
                    if i >= 1:
                        g.wait_ge(SM["aAV"], 4 * i)
                    for grp in range(16):
                        r0 = 8 * grp
                        g.dma_start(out=bfv(O_VP + 65 * grp, 64),
                                    in_=bfv(O_QKV + 2048, 1024, parts=8, p0=r0)
                                    ).then_inc(SM["aPV%d" % (grp // 4)], 16)
                    for b in range(4):
                        g.wait_ge(SM["aDIV"], 4 * i + b + 1)
                        for q in range(4):
                            grp = 4 * b + q
                            p0 = 128 * i + 8 * grp
                            g.dma_start(out=dO[:, p0:p0 + 8, :].rearrange("h p d -> p h d"),
                                        in_=bfv(O_OSB + 256 * b + 64 * q, 64)).then_inc(SM["aOW"], 16)
                for j in range(NT):
                    g.wait_ge(SM["aOW"], 256 * NT)
                    if j >= 1:
                        g.wait_ge(SM["bCF"], 10 * j)
                    g.dma_start(out=bfv(B_XB, 1024),
                                in_=dx[128 * j:128 * (j + 1), :]).then_inc(SM["bX"], 16)
                    if j >= 1:
                        g.wait_ge(SM["bOut"], j)
                        g.dma_start(out=dout[128 * (j - 1):128 * j, :],
                                    in_=f32v(B_OST, 1024)).then_inc(SM["bAdd"], 16)
                g.wait_ge(SM["bOut"], NT)
                g.dma_start(out=dout[128 * (NT - 1):128 * NT, :],
                            in_=f32v(B_OST, 1024)).then_inc(SM["bAdd"], 16)
                g.wait_ge(SM["bAdd"], 16 * NT)

    return nc


def _consts():
    m = np.zeros((128, 512), dtype=ml_dtypes.bfloat16)
    for rep in range(4):
        for p in range(8):
            m[16 * p:16 * (p + 1), 128 * rep + 16 * p:128 * rep + 16 * (p + 1)] = 1
    return np.eye(128, dtype=ml_dtypes.bfloat16), m


def kernel(L0_attn_x, L0_attn_Wqkv, L0_attn_bqkv, L0_attn_Wo, L0_attn_bo,
           L0_ffn_w1, L0_ffn_b1, L0_ffn_w2, L0_ffn_b2):
    x = np.ascontiguousarray(np.asarray(L0_attn_x, dtype=np.float32))
    ident, mask = _consts()
    nc = build_program()
    base = dict(
        wq=np.ascontiguousarray(np.asarray(L0_attn_Wqkv, np.float32)),
        bq=np.ascontiguousarray(np.asarray(L0_attn_bqkv, np.float32)).reshape(1, -1),
        wo=np.ascontiguousarray(np.asarray(L0_attn_Wo, np.float32)),
        bo=np.ascontiguousarray(np.asarray(L0_attn_bo, np.float32)).reshape(1, -1),
        w1=np.ascontiguousarray(np.asarray(L0_ffn_w1, np.float32)),
        b1=np.ascontiguousarray(np.asarray(L0_ffn_b1, np.float32)).reshape(1, -1),
        w2=np.ascontiguousarray(np.asarray(L0_ffn_w2, np.float32)),
        b2=np.ascontiguousarray(np.asarray(L0_ffn_b2, np.float32)).reshape(1, -1),
        ident=ident, mask4=mask,
    )
    in_maps = [dict(base, x=np.ascontiguousarray(x[c])) for c in range(8)]
    res = run_bass_kernel_spmd(nc, in_maps, core_ids=list(range(8)))
    out = np.stack([np.asarray(r["out"], dtype=np.float32) for r in res.results], axis=0)
    return out
